# version 03
# baseline (speedup 1.0000x reference)
import numpy as np
import contextlib
import concourse.bass as bass
import concourse.mybir as mybir
from concourse.bass_utils import run_bass_kernel_spmd

F32 = mybir.dt.float32
BF16 = mybir.dt.bfloat16
AF = mybir.ActivationFunctionType
ALU = mybir.AluOpType
AX = mybir.AxisListType

L = 2048; D = 2048; LT = 16; DT = 16
DIN = 4096; H = 64; NST = 128; PHD = 64; THF = 4096; HNF = 8192
FHID = 1792; FT = 14; NE = 8
PI = float(np.pi); TWO_PI = float(2 * np.pi); MAGIC = 12582912.0
EPS_LN = 1e-5; EPS_RMS = 1e-6; NEG = -1.0e9

_DEBUG = False


class KB:
    def __init__(self, nc, es):
        self.nc = nc
        self.sem = {}
        self.cnt = {}
        for name in ["LD", "GQ", "SQ", "PH"]:
            self._mk(es, name)
        for i in range(22):
            self._mk(es, f"D{i}")
        for nm in ["FV", "FS"]:
            self._mk(es, nm)
        for ph in range(14):
            for pre in ["T", "V", "S"]:
                self._mk(es, f"{pre}{ph}")

    def _mk(self, es, name):
        self.sem[name] = es.enter_context(self.nc.semaphore(name))
        self.cnt[name] = 0


_ALL_PLANS = []


class Plan:
    ENGS = ("sy", "gp", "te", "ve", "se")

    def __init__(self, kb):
        self.kb = kb
        self.s = {e: [] for e in self.ENGS}
        self.ev = {e: [] for e in self.ENGS}
        _ALL_PLANS.append(self)

    def op(self, e, fn, inc=None, n=1):
        if inc is None:
            self.s[e].append(fn)
            self.ev[e].append(("op",))
            return None
        self.kb.cnt[inc] += n
        h = self.kb.sem[inc]
        self.s[e].append(lambda eng, fn=fn, h=h, n=n: fn(eng).then_inc(h, n))
        self.ev[e].append(("inc", inc, n))
        return self.kb.cnt[inc]

    def tag(self, e, inc, n=1):
        self.kb.cnt[inc] += n
        h = self.kb.sem[inc]
        f = self.s[e][-1]
        self.s[e][-1] = lambda eng, f=f, h=h, n=n: f(eng).then_inc(h, n)
        self.ev[e].append(("inc", inc, n))
        return self.kb.cnt[inc]

    def fence(self, e):
        nm = {"ve": "FV", "se": "FS"}[e]
        v = self.tag(e, nm)
        self.wait(e, nm, v)

    def wait(self, e, sname, v):
        if v is None or v <= 0:
            return
        h = self.kb.sem[sname]
        self.s[e].append(lambda eng, h=h, v=v: eng.wait_ge(h, v))
        self.ev[e].append(("wait", sname, v))

    def dma(self, q, dst, src, sem=None):
        cn = sem or {"sy": "LD", "gp": "GQ", "se": "SQ"}[q]
        self.kb.cnt[cn] += 16
        h = self.kb.sem[cn]
        self.s[q].append(lambda eng, d=dst, s=src, h=h: eng.dma_start(d, s).then_inc(h, 16))
        self.ev[q].append(("inc", cn, 16))
        return self.kb.cnt[cn]

    def run(self, blk):
        st = self.s

        @blk.sync
        def _(eng):
            for f in st["sy"]:
                f(eng)

        @blk.gpsimd
        def _(eng):
            for f in st["gp"]:
                f(eng)

        @blk.tensor
        def _(eng):
            for f in st["te"]:
                f(eng)

        @blk.vector
        def _(eng):
            for f in st["ve"]:
                f(eng)

        @blk.scalar
        def _(eng):
            for f in st["se"]:
                f(eng)


def MM(out, lhsT, rhs, start=True, stop=True):
    return lambda e, o=out, l=lhsT, r=rhs, s=start, t=stop: e.matmul(o, l, r, start=s, stop=t)


def TRANS(out, in_, ident):
    return lambda e, o=out, i=in_, d=ident: e.transpose(o, i, d)


def TT(out, a, b, op):
    return lambda e, o=out, x=a, y=b, p=op: e.tensor_tensor(o, x, y, p)


def TS(out, a, s1, s2, op0, op1=None):
    if op1 is None:
        return lambda e, o=out, x=a, u=s1, p=op0: e.tensor_scalar(o, x, u, None, p)
    return lambda e, o=out, x=a, u=s1, w=s2, p=op0, q=op1: e.tensor_scalar(o, x, u, w, p, q)


def STT(out, a, s, b, op0, op1):
    return lambda e, o=out, x=a, u=s, y=b, p=op0, q=op1: e.scalar_tensor_tensor(o, x, u, y, p, q)


def ACT(out, in_, fn, bias=0.0, scale=1.0):
    return lambda e, o=out, i=in_, f=fn, b=bias, s=scale: e.activation(o, i, f, bias=b, scale=s)


def RSUM(out, in_):
    return lambda e, o=out, i=in_: e.reduce_sum(o, i, AX.X)


def RMAX(out, in_):
    return lambda e, o=out, i=in_: e.reduce_max(o, i, AX.X)


def CPY(out, in_):
    return lambda e, o=out, i=in_: e.tensor_copy(o, i)


def SCPY(out, in_):
    return lambda e, o=out, i=in_: e.copy(o, i)


def MST(ap, c):
    return lambda e, a=ap, v=c: e.memset(a, v)


def RECIP(out, in_):
    return lambda e, o=out, i=in_: e.reciprocal(o, i)


def A2(t, off, d0, d1):
    return bass.AP(t, off, [list(d0), list(d1)])


def A3(t, off, d0, d1, d2):
    return bass.AP(t, off, [list(d0), list(d1), list(d2)])



_UID = [0]


def _sbuf(nc, name, shape, dt, side=None):
    _UID[0] += 1
    return nc.sbuf_tensor(f"{name}_{_UID[0]}", list(shape), dt, side=side)


def _psum(nc, name, shape, dt):
    _UID[0] += 1
    return nc.psum_tensor(f"{name}_{_UID[0]}", list(shape), dt)


def build_nc():
    nc = bass.Bass("TRN2", target_bir_lowering=False)
    dr = {}

    def din(name, shape, dt=F32):
        dr[name] = nc.dram_tensor(name, list(shape), dt, kind="ExternalInput")

    def dsc(name, shape, dt):
        kind = "ExternalOutput" if _DEBUG else "Internal"
        dr[name] = nc.dram_tensor(name, list(shape), dt, kind=kind)

    din("x", (L, D))
    din("WxT", (D, DIN)); din("WzT", (D, DIN)); din("WBT", (D, HNF))
    din("WCT", (D, HNF)); din("WthT", (D, THF)); din("WdlT", (D, 128))
    din("WoutT", (DIN, D)); din("gwT", (128, 128))
    din("w1", (NE * D, FHID)); din("w2", (NE * FHID, D)); din("w3", (NE * D, FHID))
    din("sw1", (D, FHID)); din("sw2", (FHID, D)); din("sw3", (D, FHID))
    din("Alrep", (128, H)); din("ebrep", (128, NE))
    din("bnwB", (128, 1024)); din("bnwC", (128, 1024))
    din("biasB", (128, HNF)); din("biasC", (128, HNF))
    din("cvals", (128, 8))
    din("idF", (128, 128)); din("triL", (128, 128)); din("triUs", (128, 128))
    din("maskM", (128, 128)); din("ones128", (128, 128)); din("onesrow", (1, 128))
    din("onescol", (128, 1)); din("shSup", (128, 128)); din("shE0", (128, 128))
    dsc("sp_xp", (L, DIN), F32)
    dsc("sp_B", (L, HNF), BF16)
    dsc("sp_C", (L, HNF), BF16)
    dsc("sp_cos", (L, THF), F32)
    dsc("sp_sin", (L, THF), F32)
    dsc("sp_z", (L, DIN), F32)
    dsc("sp_y", (L, DIN), F32)
    dsc("sp_xn2", (L, D), BF16)
    dsc("sp_o2", (L, D), F32)
    dsc("sp_xn", (L, D), BF16)
    dsc("sp_rr", (L, THF), F32)
    dsc("sp_srep", (NE, 1024), F32)
    dsc("sp_acc", (L, D), F32)
    dsc("sp_x2", (L, D), F32)
    dsc("sp_wt", (NE, L), F32)
    dr["out"] = nc.dram_tensor("out", [L, D], F32, kind="ExternalOutput")

    es_all = contextlib.ExitStack()
    with es_all:
        kb = KB(nc, es_all)
        sball = lambda name, shape, dt: es_all.enter_context(_sbuf(nc, name, list(shape), dt))

        idF = sball("sb_idF", (128, 128), F32)
        idB = sball("sb_idB", (128, 128), BF16)
        triL = sball("sb_triL", (128, 128), F32)
        triUs = sball("sb_triUs", (128, 128), F32)
        maskM = sball("sb_maskM", (128, 128), F32)
        ones128 = sball("sb_ones128", (128, 128), F32)
        onesrow = sball("sb_onesrow", (1, 128), F32)
        onescol = sball("sb_onescol", (128, 1), F32)
        shSup = sball("sb_shSup", (128, 128), F32)
        shE0 = sball("sb_shE0", (128, 128), F32)
        dummy = sball("sb_dummy", (1, 8), F32)
        cvals = sball("sb_cvals", (128, 8), F32)
        for ci, cv in enumerate([EPS_LN, 1.0, EPS_RMS, PI / 2]):
            nc.const_aps.register(F32, float(cv), cvals[:, ci:ci + 1])

        def finish_phase(Pn, scope_es, waiters):
            for (sname, v) in waiters:
                Pn.wait("ve", sname, v)
            for nm in ["LD", "GQ", "SQ"] + [f"D{i}" for i in range(22)]:
                Pn.wait("ve", nm, kb.cnt[nm])
            Pn.op("ve", MST(dummy[:], 0.0), inc="PH")
            blk = scope_es.enter_context(nc.Block())
            Pn.run(blk)

        def start_phase(Pn):
            v = kb.cnt["PH"]
            for e in Plan.ENGS:
                Pn.wait(e, "PH", v)

        proj_es = contextlib.ExitStack()
        xnT = proj_es.enter_context(_sbuf(nc, "sb_xnT", [128, DT * L], BF16, side="right"))

        with contextlib.ExitStack() as ph:
            Pn = Plan(kb)
            sb = lambda n, s, d: ph.enter_context(_sbuf(nc, n, list(s), d))
            pss = lambda n, s, d: ph.enter_context(_psum(nc, n, list(s), d))
            xt = [sb(f"p0xt{j}", (128, D), F32) for j in range(2)]
            xc = sb("p0xc", (128, D), F32)
            sq = sb("p0sq", (128, D), F32)
            xnb = sb("p0xnb", (128, D), BF16)
            cst = sb("p0cst", (128, 8), F32)
            pT = pss("p0pT", (128, 2048), BF16)

            Pn.dma("sy", cvals[:], dr["cvals"][:])
            for t, name in [(idF, "idF"), (triL, "triL"), (triUs, "triUs"),
                            (maskM, "maskM"), (ones128, "ones128"), (onesrow, "onesrow"),
                            (onescol, "onescol"), (shSup, "shSup"), (shE0, "shE0")]:
                Pn.dma("sy", t[:], dr[name][:])
            gq0 = Pn.dma("gp", idB[:], dr["idF"][:])
            ldc = kb.cnt["LD"]
            Pn.wait("se", "LD", ldc)

            for i in range(LT):
                if i >= 2:
                    Pn.wait("sy", "V0", 3 * (i - 2) + 1)
                w = Pn.dma("sy", xt[i % 2][:], dr["x"][i * 128:(i + 1) * 128, :], sem=f"D{i % 2}")
                if i == 0:
                    Pn.wait("ve", "LD", ldc)
                Pn.wait("ve", f"D{i % 2}", w)
                Pn.op("ve", RSUM(cst[:, 0:1], xt[i % 2][:]))
                Pn.fence("ve")
                Pn.op("ve", TS(cst[:, 1:2], cst[:, 0:1], 1.0 / D, None, ALU.mult))
                Pn.fence("ve")
                vv = Pn.op("ve", TS(xc[:], xt[i % 2][:], cst[:, 1:2], None, ALU.subtract), inc="V0")
                Pn.wait("se", "V0", vv)
                sv = Pn.op("se", ACT(sq[:], xc[:], AF.Square), inc="S0")
                Pn.wait("ve", "S0", sv)
                vv = Pn.op("ve", RSUM(cst[:, 2:3], sq[:]), inc="V0")
                Pn.wait("se", "V0", vv)
                Pn.op("se", ACT(cst[:, 4:5], cst[:, 2:3], AF.Ln, bias=EPS_LN, scale=1.0 / D))
                Pn.fence("se")
                sv = Pn.op("se", ACT(cst[:, 5:6], cst[:, 4:5], AF.Exp, scale=-0.5), inc="S0")
                Pn.wait("ve", "S0", sv)
                vv = Pn.op("ve", TS(xnb[:], xc[:], cst[:, 5:6], None, ALU.mult), inc="V0")
                Pn.wait("te", "V0", vv)
                if i == 0:
                    Pn.wait("te", "GQ", gq0)
                else:
                    Pn.wait("te", "S0", 3 * i)
                for k in range(DT):
                    Pn.op("te", TRANS(pT[:, k * 128:(k + 1) * 128],
                                      xnb[:, k * 128:(k + 1) * 128], idB[:]))
                tm = Pn.tag("te", "T0")
                Pn.wait("se", "T0", tm)
                for k in range(DT):
                    Pn.op("se", SCPY(xnT[:, k * L + i * 128: k * L + (i + 1) * 128],
                                     pT[:, k * 128:(k + 1) * 128]))
                sv = Pn.tag("se", "S0")
                if _DEBUG:
                    Pn.wait("se", "S0", sv)
                    Pn.dma("se", dr["sp_xn"][i * 128:(i + 1) * 128, :], xnb[:])
            finish_phase(Pn, ph, [("T0", kb.cnt["T0"]), ("S0", kb.cnt["S0"]),
                                  ("V0", kb.cnt["V0"]), ("LD", kb.cnt["LD"]),
                                  ("GQ", kb.cnt["GQ"])])

        arr_es = contextlib.ExitStack()
        asb = lambda n: arr_es.enter_context(_sbuf(nc, n, [128, LT * H], F32))
        dtf = asb("dtf"); dtA = asb("dtA"); Sar = asb("Sar")
        lgc = asb("lgc"); bpr = asb("bpr"); efl = asb("efl"); gfl = asb("gfl")

        with contextlib.ExitStack() as ph:
            Pn = Plan(kb)
            start_phase(Pn)
            sb = lambda n, s, d: ph.enter_context(_sbuf(nc, n, list(s), d))
            wdl = sb("p1wdl", (128, DT * 128), BF16)
            arep = sb("p1arep", (128, H), F32)
            t0 = sb("p1t0", (128, 128), F32)
            t1 = sb("p1t1", (128, 128), F32)
            t2 = sb("p1t2", (128, 64), F32)
            pA = ph.enter_context(_psum(nc, "p1pA", [128, 512], F32))
            pB = ph.enter_context(_psum(nc, "p1pB", [128, 512], F32))

            wg = Pn.dma("gp", A3(wdl, 0, (DT * 128, 128), (128, DT), (1, 128)),
                        A3(dr["WdlT"], 0, (128, 128), (128 * 128, DT), (1, 128)))
            wl = Pn.dma("sy", arep[:], dr["Alrep"][:])
            Pn.wait("se", "LD", wl)
            sv = Pn.op("se", ACT(arep[:], arep[:], AF.Exp), inc="S1")
            Pn.wait("ve", "S1", sv)
            Pn.op("ve", TS(arep[:], arep[:], -1.0, None, ALU.mult), inc="V1")

            Pn.wait("te", "GQ", wg)
            for i in range(LT):
                if i >= 1:
                    Pn.wait("te", "V1", 1 + 3 * i)
                    Pn.wait("te", "S1", 1 + 2 * i)
                for d in range(DT):
                    Pn.op("te", MM(pA[:, 0:128],
                                   xnT[:, d * L + i * 128: d * L + (i + 1) * 128],
                                   wdl[:, d * 128:(d + 1) * 128],
                                   start=(d == 0), stop=(d == DT - 1)))
                tm = Pn.tag("te", "T1")
                dtc = dtf[:, i * H:(i + 1) * H]
                Pn.wait("ve", "T1", tm)
                Pn.op("ve", TS(t0[:, 0:64], pA[:, 0:64], -1.0, None, ALU.mult))
                Pn.fence("ve")
                vv = Pn.op("ve", TT(t0[:, 0:64], t0[:, 0:64], pA[:, 0:64], ALU.min), inc="V1")
                Pn.wait("se", "V1", vv)
                Pn.op("se", ACT(t1[:, 0:64], t0[:, 0:64], AF.Exp))
                Pn.fence("se")
                sv = Pn.op("se", ACT(t1[:, 0:64], t1[:, 0:64], AF.Ln, bias=1.0), inc="S1")
                Pn.wait("ve", "S1", sv)
                Pn.op("ve", TS(t0[:, 64:128], pA[:, 0:64], 0.0, None, ALU.max))
                Pn.fence("ve")
                vv = Pn.op("ve", TT(dtc, t0[:, 64:128], t1[:, 0:64], ALU.add), inc="V1")
                Pn.wait("se", "V1", vv)
                sv = Pn.op("se", ACT(t1[:, 64:128], pA[:, 64:128], AF.Exp, scale=-1.0), inc="S1")
                Pn.wait("ve", "S1", sv)
                Pn.op("ve", TS(t1[:, 64:128], t1[:, 64:128], 1.0, None, ALU.add))
                Pn.fence("ve")
                Pn.op("ve", RECIP(t2[:, 0:64], t1[:, 64:128]))
                Pn.fence("ve")
                Pn.op("ve", TT(dtA[:, i * H:(i + 1) * H], dtc, arep[:], ALU.mult))
                Pn.op("ve", TT(gfl[:, i * H:(i + 1) * H], t2[:, 0:64], dtc, ALU.mult))
                Pn.fence("ve")
                vv = Pn.op("ve", TT(efl[:, i * H:(i + 1) * H], dtc,
                                    gfl[:, i * H:(i + 1) * H], ALU.subtract), inc="V1")
            for i in range(LT):
                Pn.wait("te", "V1", kb.cnt["V1"])
                Pn.wait("te", "S1", kb.cnt["S1"])
                ei = efl[:, i * H:(i + 1) * H]
                Pn.op("te", MM(pB[:, 0:64], shSup[:], ei, start=True, stop=(i == LT - 1)))
                if i < LT - 1:
                    Pn.op("te", MM(pB[:, 0:64], shE0[:],
                                   efl[:, (i + 1) * H:(i + 2) * H], start=False, stop=True))
                Pn.op("te", MM(pB[:, 64:128], triUs[:], dtA[:, i * H:(i + 1) * H]))
                tm = Pn.tag("te", "T1")
                Pn.wait("ve", "T1", tm)
                Pn.op("ve", CPY(bpr[:, i * H:(i + 1) * H], pB[:, 0:64]))
                vv = Pn.op("ve", TT(t0[:, 0:64], gfl[:, i * H:(i + 1) * H],
                                    pB[:, 0:64], ALU.add), inc="V1")
                Pn.wait("se", "V1", vv)
                Pn.op("se", ACT(lgc[:, i * H:(i + 1) * H], t0[:, 0:64], AF.Ln))
                sv = Pn.op("se", ACT(t1[:, 0:64], pB[:, 64:128], AF.Exp), inc="S1")
                Pn.wait("ve", "S1", sv)
                Pn.op("ve", TT(Sar[:, i * H:(i + 1) * H], t1[:, 0:64],
                               t0[:, 0:64], ALU.mult), inc="V1")
            finish_phase(Pn, ph, [("T1", kb.cnt["T1"]), ("S1", kb.cnt["S1"]),
                                  ("V1", kb.cnt["V1"]), ("LD", kb.cnt["LD"]),
                                  ("GQ", kb.cnt["GQ"])])

        wsl = [proj_es.enter_context(_sbuf(nc, f"sb_wsl{j}", [128, DT * 1024], BF16, side="right"))
               for j in range(2)]

        def slab_src(wname, f0, fw, ocols):
            return A3(dr[wname], f0, (ocols, 128), (128 * ocols, DT), (1, fw))

        def slab_dst(j, fw):
            return A3(wsl[j], 0, (DT * fw, 128), (fw, DT), (1, fw))

        with contextlib.ExitStack() as ph:
            Pn = Plan(kb)
            start_phase(Pn)
            sb = lambda n, s, d: ph.enter_context(_sbuf(nc, n, list(s), d))
            thdt = sb("p2thdt", (128, 1024), F32)
            rr = sb("p2rr", (128, 1024), F32)
            rr2 = sb("p2rr2", (128, 1024), F32)
            carry = sb("p2carry", (1, 1024), F32)
            sst = [sb(f"p2sst{j}", (128, 1024), F32) for j in range(2)]
            cst2 = [sb(f"p2cst{j}", (128, 1024), F32) for j in range(2)]
            pP = [ph.enter_context(_psum(nc, f"p2pP{j}", [128, 1024], F32)) for j in range(2)]
            pQ = ph.enter_context(_psum(nc, "p2pQ", [128, 1024], F32))
            pCr = ph.enter_context(_psum(nc, "p2pCr", [1, 1024], F32))

            NCH = THF // 1024
            slabw = []
            for c in range(NCH):
                if c >= 2:
                    Pn.wait("gp", "T2", 2 * LT * (c - 1))
                slabw.append(Pn.dma("gp", slab_dst(c % 2, 1024), slab_src("WthT", c * 1024, 1024, THF),
                                    sem=f"D{2 + c % 2}"))

            spw = {}
            for c in range(NCH):
                Pn.wait("te", f"D{2 + c % 2}", slabw[c])
                for i in range(LT):
                    k = c * LT + i
                    if k >= 2:
                        Pn.wait("te", "V2", 2 * (k - 1) - 1)
                    for d in range(DT):
                        lhs = xnT[:, d * L + i * 128: d * L + (i + 1) * 128]
                        Pn.op("te", MM(pP[k % 2][:, 0:512], lhs, wsl[c % 2][:, d * 1024:d * 1024 + 512],
                                       start=(d == 0), stop=(d == DT - 1)))
                        Pn.op("te", MM(pP[k % 2][:, 512:1024], lhs, wsl[c % 2][:, d * 1024 + 512:(d + 1) * 1024],
                                       start=(d == 0), stop=(d == DT - 1)))
                    tm = Pn.tag("te", "T2")
                    Pn.wait("ve", "T2", tm)
                    dtb = A3(dtf, i * H + c * 16, (LT * H, 128), (1, 16), (0, 64))
                    vv = Pn.op("ve", TT(thdt[:], pP[k % 2][:], dtb, ALU.mult), inc="V2")
                    Pn.wait("te", "V2", vv)
                    Pn.op("te", MM(pQ[:, 0:512], triL[:], thdt[:, 0:512], start=True, stop=(i == 0)))
                    Pn.op("te", MM(pQ[:, 512:1024], triL[:], thdt[:, 512:1024], start=True, stop=(i == 0)))
                    if i > 0:
                        Pn.op("te", MM(pQ[:, 0:512], onesrow[:], carry[:, 0:512], start=False, stop=True))
                        Pn.op("te", MM(pQ[:, 512:1024], onesrow[:], carry[:, 512:1024], start=False, stop=True))
                    Pn.op("te", MM(pCr[:, 0:512], onescol[:], thdt[:, 0:512], start=True, stop=(i == 0)))
                    Pn.op("te", MM(pCr[:, 512:1024], onescol[:], thdt[:, 512:1024], start=True, stop=(i == 0)))
                    if i > 0:
                        Pn.op("te", MM(pCr[:, 0:512], onesrow[0:1, 0:1], carry[:, 0:512], start=False, stop=True))
                        Pn.op("te", MM(pCr[:, 512:1024], onesrow[0:1, 0:1], carry[:, 512:1024], start=False, stop=True))
                    tm2 = Pn.tag("te", "T2")
                    Pn.wait("ve", "T2", tm2)
                    Pn.op("ve", CPY(carry[:, :], pCr[:, :]))
                    Pn.op("ve", TS(rr[:], pQ[:], 1.0 / TWO_PI, MAGIC, ALU.mult, ALU.add))
                    Pn.fence("ve")
                    Pn.op("ve", TS(rr[:], rr[:], MAGIC, None, ALU.subtract))
                    Pn.fence("ve")
                    Pn.op("ve", STT(rr[:], rr[:], -TWO_PI, pQ[:], ALU.mult, ALU.add))
                    Pn.op("ve", TS(rr2[:], pQ[:], 1.0 / TWO_PI, MAGIC + 0.25, ALU.mult, ALU.add))
                    Pn.fence("ve")
                    Pn.op("ve", TS(rr2[:], rr2[:], MAGIC, None, ALU.subtract))
                    Pn.fence("ve")
                    vv = Pn.op("ve", STT(rr2[:], rr2[:], -TWO_PI, pQ[:], ALU.mult, ALU.add), inc="V2")
                    Pn.wait("se", "V2", vv)
                    if k >= 2:
                        Pn.wait("se", f"D{13 + k % 2}", spw[k - 2])
                    Pn.op("se", ACT(sst[k % 2][:], rr[:], AF.Sin))
                    sv = Pn.op("se", ACT(cst2[k % 2][:], rr2[:], AF.Sin, bias=PI / 2), inc="S2")
                    Pn.wait("se", "S2", sv)
                    if _DEBUG:
                        Pn.dma("se", dr["sp_rr"][i * 128:(i + 1) * 128, c * 1024:(c + 1) * 1024], rr[:])
                        Pn.wait("ve", "SQ", kb.cnt["SQ"])
                    Pn.dma("se", dr["sp_sin"][i * 128:(i + 1) * 128, c * 1024:(c + 1) * 1024],
                           sst[k % 2][:], sem=f"D{13 + k % 2}")
                    spw[k] = Pn.dma("se", dr["sp_cos"][i * 128:(i + 1) * 128, c * 1024:(c + 1) * 1024],
                                    cst2[k % 2][:], sem=f"D{13 + k % 2}")
            finish_phase(Pn, ph, [("T2", kb.cnt["T2"]), ("V2", kb.cnt["V2"]),
                                  ("S2", kb.cnt["S2"]), ("SQ", kb.cnt["SQ"]),
                                  ("GQ", kb.cnt["GQ"])])

        def bc_phase(Wn, bnwn, biasn, spn, TSM, VSM, SSM):
            with contextlib.ExitStack() as ph:
                Pn = Plan(kb)
                start_phase(Pn)
                sb = lambda n, s, d: ph.enter_context(_sbuf(nc, n, list(s), d))
                sq = sb("bcsq", (128, 1024), F32)
                tn = sb("bctn", (128, 1024), F32)
                msc = sb("bcmsc", (128, 24), F32)
                bnw = sb("bcbnw", (128, 1024), F32)
                brep = [sb(f"bcbrep{j}", (128, 1024), F32) for j in range(2)]
                cosb = [sb(f"bccos{j}", (128, 512), F32) for j in range(2)]
                sinb = [sb(f"bcsin{j}", (128, 512), F32) for j in range(2)]
                m1 = sb("bcm1", (128, 512), F32)
                m2 = sb("bcm2", (128, 512), F32)
                stg = [sb(f"bcstg{j}", (128, 1024), BF16) for j in range(2)]
                stgT = [sb(f"bcstgT{j}", (128, 1024), BF16) for j in range(2)]
                pP = [ph.enter_context(_psum(nc, f"bcpP{j}", [128, 1024], F32)) for j in range(2)]
                pTt = [ph.enter_context(_psum(nc, f"bcpTt{j}", [128, 1024], BF16)) for j in range(2)]

                Pn.dma("sy", bnw[:], dr[bnwn][:])
                NCH = HNF // 1024
                slabw, brw, csw = [], [], []
                for c in range(NCH):
                    if c >= 2:
                        Pn.wait("gp", TSM, 2 * LT * (c - 1))
                    slabw.append(Pn.dma("gp", slab_dst(c % 2, 1024), slab_src(Wn, c * 1024, 1024, HNF),
                                        sem=f"D{2 + c % 2}"))
                brw = [Pn.dma("sy", brep[0][:], dr[biasn][:, 0:1024], sem="D4"),
                       Pn.dma("sy", brep[1][:], dr[biasn][:, 1024:2048], sem="D5")]
                for c in range(NCH):
                    h0 = c * 64
                    for i in range(LT):
                        k = c * LT + i
                        if k >= 2:
                            Pn.wait("sy", VSM, 3 * (k - 1))
                        Pn.dma("sy", cosb[k % 2][:], dr["sp_cos"][i * 128:(i + 1) * 128, h0 * 8:h0 * 8 + 512],
                               sem=f"D{6 + k % 2}")
                        csw.append(Pn.dma("sy", sinb[k % 2][:], dr["sp_sin"][i * 128:(i + 1) * 128, h0 * 8:h0 * 8 + 512],
                                          sem=f"D{6 + k % 2}"))
                        if i == LT - 1 and c + 2 < NCH:
                            Pn.wait("sy", VSM, 3 * LT * (c + 1))
                            brw.append(Pn.dma("sy", brep[c % 2][:],
                                              dr[biasn][:, (c + 2) * 1024:(c + 3) * 1024],
                                              sem=f"D{4 + c % 2}"))

                spw = {}
                for c in range(NCH):
                    Pn.wait("te", f"D{2 + c % 2}", slabw[c])
                    for i in range(LT):
                        k = c * LT + i
                        if k >= 2:
                            Pn.wait("te", VSM, 3 * (k - 1))
                        for d in range(DT):
                            lhs = xnT[:, d * L + i * 128: d * L + (i + 1) * 128]
                            Pn.op("te", MM(pP[k % 2][:, 0:512], lhs,
                                           wsl[c % 2][:, d * 1024:d * 1024 + 512],
                                           start=(d == 0), stop=(d == DT - 1)))
                            Pn.op("te", MM(pP[k % 2][:, 512:1024], lhs,
                                           wsl[c % 2][:, d * 1024 + 512:(d + 1) * 1024],
                                           start=(d == 0), stop=(d == DT - 1)))
                        tm = Pn.tag("te", TSM)
                        Pn.wait("se", TSM, tm)
                        sv = Pn.op("se", ACT(sq[:], pP[k % 2][:], AF.Square), inc=SSM)
                        Pn.wait("ve", SSM, sv)
                        for hh in range(8):
                            Pn.op("ve", RSUM(msc[:, hh:hh + 1], sq[:, hh * 128:(hh + 1) * 128]))
                        vv = Pn.tag("ve", VSM)
                        Pn.wait("se", VSM, vv)
                        Pn.op("se", ACT(msc[:, 9:17], msc[:, 0:8], AF.Ln, bias=EPS_RMS, scale=1.0 / NST))
                        Pn.fence("se")
                        sv = Pn.op("se", ACT(msc[:, 9:17], msc[:, 9:17], AF.Exp, scale=-0.5), inc=SSM)
                        Pn.wait("ve", SSM, sv)
                        rstdb = A3(msc, 9, (24, 128), (1, 8), (0, 128))
                        Pn.op("ve", TT(tn[:], pP[k % 2][:], rstdb, ALU.mult))
                        Pn.fence("ve")
                        Pn.op("ve", TT(tn[:], tn[:], bnw[:], ALU.mult))
                        Pn.fence("ve")
                        Pn.wait("ve", f"D{4 + c % 2}", brw[c])
                        vv = Pn.op("ve", TT(tn[:], tn[:], brep[c % 2][:], ALU.add), inc=VSM)
                        Pn.wait("ve", VSM, vv)
                        Pn.wait("ve", f"D{6 + k % 2}", csw[k])
                        ev_in = A2(tn, 0, (1024, 128), (2, 512))
                        od_in = A2(tn, 1, (1024, 128), (2, 512))
                        ev_out = A2(stg[k % 2], 0, (1024, 128), (2, 512))
                        od_out = A2(stg[k % 2], 1, (1024, 128), (2, 512))
                        if k >= 2:
                            Pn.wait("ve", TSM, ttw[k - 2])
                        Pn.op("ve", TT(m1[:], ev_in, cosb[k % 2][:], ALU.mult))
                        Pn.op("ve", TT(m2[:], od_in, sinb[k % 2][:], ALU.mult))
                        Pn.fence("ve")
                        Pn.op("ve", TT(ev_out, m1[:], m2[:], ALU.subtract))
                        Pn.op("ve", TT(m1[:], ev_in, sinb[k % 2][:], ALU.mult))
                        Pn.op("ve", TT(m2[:], od_in, cosb[k % 2][:], ALU.mult))
                        Pn.fence("ve")
                        vv = Pn.op("ve", TT(od_out, m1[:], m2[:], ALU.add), inc=VSM)
                        Pn.wait("te", VSM, vv)
                        if k >= 2:
                            Pn.wait("te", SSM, 3 * (k - 1))
                        for hl in range(8):
                            Pn.op("te", TRANS(pTt[k % 2][:, hl * 128:(hl + 1) * 128],
                                              stg[k % 2][:, hl * 128:(hl + 1) * 128], idB[:]))
                        ttw[k] = Pn.tag("te", TSM)
                        Pn.wait("se", TSM, ttw[k])
                        if k >= 2:
                            Pn.wait("se", f"D{13 + k % 2}", spw[k - 2])
                        for hl in range(8):
                            Pn.op("se", SCPY(stgT[k % 2][:, hl * 128:(hl + 1) * 128],
                                             pTt[k % 2][:, hl * 128:(hl + 1) * 128]))
                        sv = Pn.tag("se", SSM)
                        Pn.wait("se", SSM, sv)
                        spw[k] = Pn.dma("se",
                                        A2(dr[spn], i * 128 * HNF + c * 1024, (HNF, 128), (1, 1024)),
                                        stgT[k % 2][:], sem=f"D{13 + k % 2}")
                finish_phase(Pn, ph, [(TSM, kb.cnt[TSM]), (VSM, kb.cnt[VSM]),
                                      (SSM, kb.cnt[SSM]), ("SQ", kb.cnt["SQ"]),
                                      ("GQ", kb.cnt["GQ"]), ("LD", kb.cnt["LD"])])

        ttw = {}
        bc_phase("WBT", "bnwB", "biasB", "sp_B", "T3", "V3", "S3")
        bc_phase("WCT", "bnwC", "biasC", "sp_C", "T4", "V4", "S4")

        with contextlib.ExitStack() as ph:
            Pn = Plan(kb)
            start_phase(Pn)
            stg = [ph.enter_context(_sbuf(nc, f"p5stg{j}", [128, 1024], F32)) for j in range(2)]
            pP = [ph.enter_context(_psum(nc, f"p5pP{j}", [128, 1024], F32)) for j in range(2)]
            NCH = DIN // 1024
            slabw = []
            for c in range(NCH):
                if c >= 2:
                    Pn.wait("gp", "T5", LT * (c - 1))
                slabw.append(Pn.dma("gp", slab_dst(c % 2, 1024), slab_src("WxT", c * 1024, 1024, DIN),
                                    sem=f"D{2 + c % 2}"))
            spw = {}
            for c in range(NCH):
                Pn.wait("te", f"D{2 + c % 2}", slabw[c])
                for i in range(LT):
                    k = c * LT + i
                    if k >= 2:
                        Pn.wait("te", "S5", k - 1)
                    for d in range(DT):
                        lhs = xnT[:, d * L + i * 128: d * L + (i + 1) * 128]
                        Pn.op("te", MM(pP[k % 2][:, 0:512], lhs, wsl[c % 2][:, d * 1024:d * 1024 + 512],
                                       start=(d == 0), stop=(d == DT - 1)))
                        Pn.op("te", MM(pP[k % 2][:, 512:1024], lhs, wsl[c % 2][:, d * 1024 + 512:(d + 1) * 1024],
                                       start=(d == 0), stop=(d == DT - 1)))
                    tm = Pn.tag("te", "T5")
                    Pn.wait("se", "T5", tm)
                    if k >= 2:
                        Pn.wait("se", f"D{13 + k % 2}", spw[k - 2])
                    sv = Pn.op("se", SCPY(stg[k % 2][:], pP[k % 2][:]), inc="S5")
                    Pn.wait("se", "S5", sv)
                    spw[k] = Pn.dma("se", dr["sp_xp"][i * 128:(i + 1) * 128, c * 1024:(c + 1) * 1024],
                                    stg[k % 2][:], sem=f"D{13 + k % 2}")
            finish_phase(Pn, ph, [("T5", kb.cnt["T5"]), ("S5", kb.cnt["S5"]),
                                  ("SQ", kb.cnt["SQ"]), ("GQ", kb.cnt["GQ"])])

        with contextlib.ExitStack() as ph:
            Pn = Plan(kb)
            start_phase(Pn)
            zsl = [ph.enter_context(_sbuf(nc, f"p6zsl{j}", [128, DT * 128], BF16)) for j in range(2)]
            stg = [ph.enter_context(_sbuf(nc, f"p6stg{j}", [128, 512], F32)) for j in range(2)]
            pZ = [ph.enter_context(_psum(nc, f"p6pZ{j}", [128, 512], F32)) for j in range(2)]
            NFT = DIN // 128
            slabw = []
            for ft in range(NFT):
                if ft >= 2:
                    Pn.wait("gp", "T6", 4 * (ft - 1))
                slabw.append(Pn.dma("gp",
                                    A3(zsl[ft % 2], 0, (DT * 128, 128), (128, DT), (1, 128)),
                                    A3(dr["WzT"], ft * 128, (DIN, 128), (128 * DIN, DT), (1, 128)),
                                    sem=f"D{2 + ft % 2}"))
            spw = {}
            for ft in range(NFT):
                Pn.wait("te", f"D{2 + ft % 2}", slabw[ft])
                for lc in range(4):
                    k = ft * 4 + lc
                    if k >= 2:
                        Pn.wait("te", "S6", k - 1)
                    for d in range(DT):
                        Pn.op("te", MM(pZ[k % 2][:], zsl[ft % 2][:, d * 128:(d + 1) * 128],
                                       xnT[:, d * L + lc * 512:d * L + (lc + 1) * 512],
                                       start=(d == 0), stop=(d == DT - 1)))
                    tm = Pn.tag("te", "T6")
                    Pn.wait("se", "T6", tm)
                    if k >= 2:
                        Pn.wait("se", f"D{13 + k % 2}", spw[k - 2])
                    sv = Pn.op("se", SCPY(stg[k % 2][:], pZ[k % 2][:]), inc="S6")
                    Pn.wait("se", "S6", sv)
                    for b in range(4):
                        lt = lc * 4 + b
                        spw[k] = Pn.dma("se",
                                        A2(dr["sp_z"], lt * 128 * DIN + ft * 128, (DIN, 128), (1, 128)),
                                        A2(stg[k % 2], b * 128, (512, 128), (1, 128)),
                                        sem=f"D{13 + k % 2}")
            finish_phase(Pn, ph, [("T6", kb.cnt["T6"]), ("S6", kb.cnt["S6"]),
                                  ("SQ", kb.cnt["SQ"]), ("GQ", kb.cnt["GQ"])])

        proj_es.close()

        with contextlib.ExitStack() as ph:
            Pn = Plan(kb)
            start_phase(Pn)
            sb = lambda n, s, d: ph.enter_context(_sbuf(nc, n, list(s), d))
            Bt = [sb(f"scB{j}", (128, HNF), BF16) for j in range(2)]
            Ct = [sb(f"scC{j}", (128, HNF), BF16) for j in range(2)]
            Xt = [sb(f"scX{j}", (128, DIN), F32) for j in range(2)]
            Hf = sb("scHf", (128, H * PHD), F32)
            yst = sb("scyst", (128, DIN), F32)
            V0 = [sb(f"scV0{j}", (128, 128), F32) for j in range(2)]
            KT = [sb(f"scKT{j}", (128, 128), F32) for j in range(2)]
            Dsb = [sb(f"scD{j}", (128, 128), F32) for j in range(2)]
            Esb = [sb(f"scE{j}", (128, 128), F32) for j in range(2)]
            Rsb = [sb(f"scR{j}", (128, 128), F32) for j in range(2)]
            Wsb = [sb(f"scW{j}", (128, 128), F32) for j in range(2)]
            Cti = [sb(f"scCt{j}", (128, 128), F32) for j in range(2)]
            tmpI = [sb(f"scI{j}", (128, 128), F32) for j in range(2)]
            Htm = [sb(f"scHt{j}", (128, PHD), F32) for j in range(2)]
            pG = [ph.enter_context(_psum(nc, f"scpG{j}", [128, 512], F32)) for j in range(2)]
            pTK = [ph.enter_context(_psum(nc, f"scpTK{j}", [128, 1024], BF16)) for j in range(2)]
            pDR = [ph.enter_context(_psum(nc, f"scpDR{j}", [128, 512], F32)) for j in range(2)]
            pYH = [ph.enter_context(_psum(nc, f"scpYH{j}", [128, 512], F32)) for j in range(2)]

            NQ = LT
            base_T, base_V, base_S = kb.cnt["T7"], kb.cnt["V7"], kb.cnt["S7"]
            v_init = Pn.op("ve", MST(Hf[:], 0.0), inc="V7")
            TPI, VPI, SPI = 4, 5, 3

            def t_(idx, j):
                return base_T + idx * TPI + j

            def v_(idx, j):
                return base_V + 1 + idx * VPI + j

            def s_(idx, j):
                return base_S + idx * SPI + j

            ldw = []
            for q in range(NQ):
                if q >= 2:
                    Pn.wait("sy", "T7", t_((q - 1) * H - 1, TPI))
                Pn.dma("sy", Bt[q % 2][:], dr["sp_B"][q * 128:(q + 1) * 128, :], sem=f"D{q % 2}")
                Pn.dma("sy", Ct[q % 2][:], dr["sp_C"][q * 128:(q + 1) * 128, :], sem=f"D{q % 2}")
                ldw.append(Pn.dma("sy", Xt[q % 2][:], dr["sp_xp"][q * 128:(q + 1) * 128, :], sem=f"D{q % 2}"))

            spillw = [0] * NQ
            for q in range(NQ):
                qp = q % 2
                for h in range(H):
                    idx = q * H + h
                    p = h % 2
                    col = q * H + h
                    Ksl = Bt[qp][:, h * 128:(h + 1) * 128]
                    Csl = Ct[qp][:, h * 128:(h + 1) * 128]
                    Xsl = Xt[qp][:, h * 64:(h + 1) * 64]
                    if idx == 0:
                        Pn.wait("te", "D0", ldw[0])
                        Pn.wait("te", "V7", v_init)
                    if h == 0 and q > 0:
                        Pn.wait("te", f"D{q % 2}", ldw[q])
                    if idx >= 2:
                        Pn.wait("te", "S7", s_(idx - 2, SPI))
                        Pn.wait("te", "V7", v_(idx - 2, VPI))
                    Pn.op("te", MM(pG[p][:, 0:128], Ksl, Csl))
                    Pn.tag("te", "T7")
                    Pn.op("te", TRANS(pTK[p][:, 0:128], Ksl, idB[:]))
                    Pn.tag("te", "T7")
                    Pn.wait("te", "V7", v_(idx, 1))
                    Pn.op("te", MM(pDR[p][:, 0:128], triUs[:], V0[p][:]))
                    Pn.op("te", MM(pDR[p][:, 128:256], ones128[:], V0[p][:]))
                    Pn.tag("te", "T7")
                    Pn.wait("te", "V7", v_(idx, 4))
                    Pn.wait("te", "S7", s_(idx, 1))
                    Pn.op("te", MM(pYH[p][0:64, 0:128], Xsl, Wsb[p][:], start=True, stop=False))
                    Pn.op("te", MM(pYH[p][0:64, 0:128], Hf[:, h * PHD:(h + 1) * PHD],
                                   Cti[p][:], start=False, stop=True))
                    Pn.op("te", MM(pYH[p][:, 128:192], KT[p][:], Xsl))
                    Pn.tag("te", "T7")
                    if idx == 0:
                        Pn.wait("ve", "D0", ldw[0])
                    if h == 0 and q > 0:
                        Pn.wait("ve", f"D{q % 2}", ldw[q])
                    if idx >= 2:
                        Pn.wait("ve", "T7", t_(idx - 2, TPI))
                        Pn.wait("ve", "S7", s_(idx - 2, SPI))
                    Pn.op("ve", TS(V0[p][:], triL[:], dtA[:, col:col + 1], None, ALU.mult), inc="V7")
                    Pn.wait("ve", "T7", t_(idx, 3))
                    Pn.op("ve", TT(Dsb[p][:], pDR[p][:, 0:128], maskM[:], ALU.add), inc="V7")
                    Pn.wait("ve", "S7", s_(idx, 2))
                    Pn.op("ve", TS(tmpI[p][:], idF[:], bpr[:, col:col + 1], None, ALU.mult))
                    Pn.fence("ve")
                    Pn.op("ve", TT(Esb[p][:], Esb[p][:], tmpI[p][:], ALU.subtract))
                    Pn.fence("ve")
                    Pn.op("ve", TT(Wsb[p][:], Esb[p][:], pG[p][:, 0:128], ALU.mult), inc="V7")
                    Pn.op("ve", TT(Cti[p][:], Csl, Rsb[p][:], ALU.mult), inc="V7")
                    Pn.wait("ve", "T7", t_(idx, 4))
                    Pn.op("ve", TS(Htm[p][:], Hf[:, h * PHD:(h + 1) * PHD],
                                   Rsb[p][:, 127:128], None, ALU.mult))
                    Pn.fence("ve")
                    Pn.op("ve", TT(Hf[:, h * PHD:(h + 1) * PHD], Htm[p][:],
                                   pYH[p][:, 128:192], ALU.add), inc="V7")
                    if idx >= 2:
                        Pn.wait("se", "T7", t_(idx - 2, TPI))
                        Pn.wait("se", "V7", v_(idx - 2, VPI))
                    if h == 0 and q >= 1:
                        Pn.wait("se", "D15", spillw[q - 1])
                    Pn.wait("se", "T7", t_(idx, 2))
                    Pn.op("se", (lambda e, o=KT[p][:], i_=pTK[p][:, 0:128], sc=Sar[:, col:col + 1]:
                                 e.activation(o, i_, AF.Copy, scale=sc)))
                    Pn.tag("se", "S7")
                    Pn.wait("se", "V7", v_(idx, 2))
                    Pn.wait("se", "T7", t_(idx, 3))
                    Pn.op("se", ACT(Esb[p][:], Dsb[p][:], AF.Exp, bias=lgc[:, col:col + 1]))
                    Pn.op("se", ACT(Rsb[p][:], pDR[p][:, 128:256], AF.Exp))
                    Pn.tag("se", "S7")
                    Pn.wait("se", "T7", t_(idx, 4))
                    Pn.op("se", SCPY(yst[p * 64:(p + 1) * 64, (h // 2) * 128:(h // 2 + 1) * 128],
                                     pYH[p][0:64, 0:128]))
                    Pn.tag("se", "S7")
                Pn.wait("se", "S7", s_(q * H + H - 1, SPI))
                spillw[q] = Pn.dma("se", A2(dr["sp_y"], q * 128 * DIN, (DIN, 128), (1, DIN)),
                                   yst[:], sem="D15")
            assert kb.cnt["T7"] == base_T + NQ * H * TPI
            assert kb.cnt["V7"] == base_V + 1 + NQ * H * VPI
            assert kb.cnt["S7"] == base_S + NQ * H * SPI
            finish_phase(Pn, ph, [("T7", kb.cnt["T7"]), ("V7", kb.cnt["V7"]),
                                  ("S7", kb.cnt["S7"]), ("SQ", kb.cnt["SQ"]),
                                  ("LD", kb.cnt["LD"])])

        arr_es.close()

        with contextlib.ExitStack() as ph:
            Pn = Plan(kb)
            start_phase(Pn)
            sb = lambda n, s, d: ph.enter_context(_sbuf(nc, n, list(s), d))
            wsl8 = sb("p8wsl", (128, 32 * 1024), BF16)
            ytb = [sb(f"p8y{j}", (128, DIN), F32) for j in range(2)]
            ztb = [sb(f"p8z{j}", (128, DIN), F32) for j in range(2)]
            zs = sb("p8zs", (128, DIN), BF16)
            vst = [sb(f"p8v{j}", (128, DIN), BF16) for j in range(2)]
            o2s = [sb(f"p8o{j}", (128, 1024), F32) for j in range(2)]
            pO = [ph.enter_context(_psum(nc, f"p8pO{j}", [128, 1024], F32)) for j in range(2)]

            base_T = kb.cnt["T8"]
            spw = {}
            for dh in range(2):
                if dh == 1:
                    Pn.wait("gp", "T8", base_T + LT)
                wq = Pn.dma("gp", A3(wsl8, 0, (32 * 1024, 128), (1024, 32), (1, 1024)),
                            A3(dr["WoutT"], dh * 1024, (D, 128), (128 * D, 32), (1, 1024)))
                ldy = []
                for lt in range(LT):
                    k = dh * LT + lt
                    if k >= 2:
                        Pn.wait("sy", "V8", 2 * (k - 1) - 1)
                    Pn.dma("sy", ytb[lt % 2][:], dr["sp_y"][lt * 128:(lt + 1) * 128, :], sem=f"D{k % 2}")
                    ldy.append(Pn.dma("sy", ztb[lt % 2][:], dr["sp_z"][lt * 128:(lt + 1) * 128, :],
                                      sem=f"D{k % 2}"))
                for lt in range(LT):
                    k = dh * LT + lt
                    Pn.wait("se", f"D{k % 2}", ldy[lt])
                    if k >= 1:
                        Pn.wait("se", "V8", 2 * (k - 1) + 1)
                    sv = Pn.op("se", ACT(zs[:], ztb[lt % 2][:], AF.Silu), inc="S8")
                    Pn.wait("ve", "S8", sv)
                    if k >= 2:
                        Pn.wait("ve", "T8", base_T + k - 1)
                    vv = Pn.op("ve", TT(vst[lt % 2][:], ytb[lt % 2][:], zs[:], ALU.mult), inc="V8")
                    Pn.wait("te", "V8", vv)
                    if lt == 0:
                        Pn.wait("te", "GQ", wq)
                    if k >= 2:
                        Pn.wait("te", "V8", 2 * (k - 1))
                    for d2 in range(32):
                        lhs = vst[lt % 2][:, d2 * 128:(d2 + 1) * 128]
                        Pn.op("te", MM(pO[k % 2][:, 0:512], lhs, wsl8[:, d2 * 1024:d2 * 1024 + 512],
                                       start=(d2 == 0), stop=(d2 == 31)))
                        Pn.op("te", MM(pO[k % 2][:, 512:1024], lhs, wsl8[:, d2 * 1024 + 512:(d2 + 1) * 1024],
                                       start=(d2 == 0), stop=(d2 == 31)))
                    tm = Pn.tag("te", "T8")
                    Pn.wait("ve", "T8", tm)
                    if k >= 2:
                        Pn.wait("ve", f"D{13 + k % 2}", spw[k - 2])
                    vv = Pn.op("ve", CPY(o2s[k % 2][:], pO[k % 2][:]), inc="V8")
                    Pn.wait("se", "V8", vv)
                    spw[k] = Pn.dma("se", dr["sp_o2"][lt * 128:(lt + 1) * 128, dh * 1024:(dh + 1) * 1024],
                                    o2s[k % 2][:], sem=f"D{13 + k % 2}")
            finish_phase(Pn, ph, [("T8", kb.cnt["T8"]), ("V8", kb.cnt["V8"]),
                                  ("S8", kb.cnt["S8"]), ("SQ", kb.cnt["SQ"]),
                                  ("GQ", kb.cnt["GQ"]), ("LD", kb.cnt["LD"])])

        with contextlib.ExitStack() as ph:
            Pn = Plan(kb)
            start_phase(Pn)
            sb = lambda n, s, d: ph.enter_context(_sbuf(nc, n, list(s), d))
            o2t = [sb(f"p8bo{j}", (128, D), F32) for j in range(2)]
            xt = [sb(f"p8bx{j}", (128, D), F32) for j in range(2)]
            xc = sb("p8bxc", (128, D), F32)
            sq = sb("p8bsq", (128, D), F32)
            x2t = sb("p8bx2", (128, D), F32)
            xnb = sb("p8bxnb", (128, D), BF16)
            xn2f = sb("p8bxn2f", (128, D), F32)
            xfst = sb("p8bxfst", (128, D), F32)
            stg = sb("p8bstg", (128, D), BF16)
            cst = sb("p8bcst", (128, 8), F32)
            gwF = sb("p8bgw", (128, 128), F32)
            ebr = sb("p8bebr", (128, 8), F32)
            sc = sb("p8bsc", (128, 48), F32)
            wts = sb("p8bwts", (128, 8), F32)
            wtT = sb("p8bwtT", (8, L), F32)
            pT = ph.enter_context(_psum(nc, "p8bpT", [128, 2048], BF16))
            pTf = ph.enter_context(_psum(nc, "p8bpTf", [128, 2048], F32))
            pGt = ph.enter_context(_psum(nc, "p8bpGt", [128, 512], F32))
            wgf = Pn.dma("sy", gwF[:], dr["gwT"][:])
            wef = Pn.dma("sy", ebr[:], dr["ebrep"][:])

            def ln_into(src_ap, out_ap, out_is_bf):
                Pn.op("ve", RSUM(cst[:, 0:1], src_ap))
                Pn.fence("ve")
                Pn.op("ve", TS(cst[:, 1:2], cst[:, 0:1], 1.0 / D, None, ALU.mult))
                Pn.fence("ve")
                vv = Pn.op("ve", TS(xc[:], src_ap, cst[:, 1:2], None, ALU.subtract), inc="V9")
                Pn.wait("se", "V9", vv)
                sv = Pn.op("se", ACT(sq[:], xc[:], AF.Square), inc="S9")
                Pn.wait("ve", "S9", sv)
                vv = Pn.op("ve", RSUM(cst[:, 2:3], sq[:]), inc="V9")
                Pn.wait("se", "V9", vv)
                Pn.op("se", ACT(cst[:, 4:5], cst[:, 2:3], AF.Ln, bias=EPS_LN, scale=1.0 / D))
                Pn.fence("se")
                sv = Pn.op("se", ACT(cst[:, 5:6], cst[:, 4:5], AF.Exp, scale=-0.5), inc="S9")
                Pn.wait("ve", "S9", sv)
                return Pn.op("ve", TS(out_ap, xc[:], cst[:, 5:6], None, ALU.mult), inc="V9")

            spx = {}
            spn = {}
            wtv = {}
            ve_tile = 14
            se_tile = 7
            for i in range(LT):
                if i >= 2:
                    Pn.wait("sy", "V9", ve_tile * (i - 2) + 4)
                Pn.dma("sy", o2t[i % 2][:], dr["sp_o2"][i * 128:(i + 1) * 128, :], sem=f"D{i % 2}")
                w2_ = Pn.dma("sy", xt[i % 2][:], dr["x"][i * 128:(i + 1) * 128, :], sem=f"D{i % 2}")
                Pn.wait("ve", f"D{i % 2}", w2_)
                lnv = ln_into(o2t[i % 2][:], xc[:], False)
                if i >= 1:
                    Pn.wait("ve", "D17", spx[i - 1])
                Pn.wait("ve", "V9", lnv)
                vv = Pn.op("ve", TT(x2t[:], xt[i % 2][:], xc[:], ALU.add), inc="V9")
                Pn.wait("se", "V9", vv)
                spx[i] = Pn.dma("se", dr["sp_x2"][i * 128:(i + 1) * 128, :], x2t[:], sem="D17")
                vv = ln_into(x2t[:], xn2f[:], False)
                Pn.wait("ve", "V9", vv)
                vv = Pn.op("ve", CPY(xnb[:], xn2f[:]), inc="V9")
                Pn.wait("te", "V9", vv)
                if i >= 1:
                    Pn.wait("te", "S9", se_tile * i)
                for k in range(DT):
                    Pn.op("te", TRANS(pT[:, k * 128:(k + 1) * 128],
                                      xnb[:, k * 128:(k + 1) * 128], idB[:]))
                tm = Pn.tag("te", "T9")
                for k in range(DT):
                    Pn.op("te", TRANS(pTf[:, k * 128:(k + 1) * 128],
                                      xn2f[:, k * 128:(k + 1) * 128], idF[:]))
                tmf = Pn.tag("te", "T9")
                Pn.wait("se", "T9", tm)
                if i >= 1:
                    Pn.wait("se", "D18", spn[i - 1])
                for k in range(DT):
                    Pn.op("se", SCPY(stg[:, k * 128:(k + 1) * 128], pT[:, k * 128:(k + 1) * 128]))
                sv = Pn.tag("se", "S9")
                Pn.wait("se", "S9", sv)
                spn[i] = Pn.dma("se", dr["sp_xn2"][i * 128:(i + 1) * 128, :], stg[:], sem="D18")
                Pn.wait("se", "T9", tmf)
                for k in range(DT):
                    Pn.op("se", SCPY(xfst[:, k * 128:(k + 1) * 128], pTf[:, k * 128:(k + 1) * 128]))
                sv = Pn.tag("se", "S9")
                Pn.wait("te", "S9", sv)
                Pn.wait("te", "LD", wgf)
                if i >= 1:
                    Pn.wait("te", "V9", wtv[i - 1])
                for d in range(DT):
                    Pn.op("te", MM(pGt[:, 0:8], xfst[:, d * 128:(d + 1) * 128],
                                   gwF[:, d * 8:(d + 1) * 8],
                                   start=(d == 0), stop=(d == DT - 1)))
                tg = Pn.tag("te", "T9")
                Pn.wait("ve", "LD", wef)
                Pn.wait("ve", "T9", tg)
                Pn.op("ve", RMAX(sc[:, 0:1], pGt[:, 0:8]))
                Pn.fence("ve")
                vv = Pn.op("ve", TS(sc[:, 1:2], sc[:, 0:1], -1.0, None, ALU.mult), inc="V9")
                Pn.wait("se", "V9", vv)
                sv = Pn.op("se", ACT(sc[:, 8:16], pGt[:, 0:8], AF.Exp, bias=sc[:, 1:2]), inc="S9")
                Pn.wait("ve", "S9", sv)
                Pn.op("ve", RSUM(sc[:, 2:3], sc[:, 8:16]))
                Pn.fence("ve")
                Pn.op("ve", RECIP(sc[:, 3:4], sc[:, 2:3]))
                Pn.fence("ve")
                vv = Pn.op("ve", TS(sc[:, 16:24], sc[:, 8:16], sc[:, 3:4], None, ALU.mult), inc="V9")
                Pn.wait("ve", "V9", vv)
                Pn.op("ve", TT(sc[:, 24:32], sc[:, 16:24], ebr[:], ALU.add))
                Pn.fence("ve")
                Pn.op("ve", RMAX(sc[:, 4:5], sc[:, 24:32]))
                Pn.fence("ve")
                vv = Pn.op("ve", TS(sc[:, 32:40], sc[:, 24:32], sc[:, 4:5], None, ALU.is_ge), inc="V9")
                Pn.wait("ve", "V9", vv)
                Pn.op("ve", STT(sc[:, 40:48], sc[:, 32:40], NEG, sc[:, 24:32], ALU.mult, ALU.add))
                Pn.fence("ve")
                Pn.op("ve", RMAX(sc[:, 5:6], sc[:, 40:48]))
                Pn.fence("ve")
                vv = Pn.op("ve", TS(sc[:, 32:40], sc[:, 24:32], sc[:, 5:6], None, ALU.is_ge), inc="V9")
                Pn.wait("ve", "V9", vv)
                vv = Pn.op("ve", TT(wts[:], sc[:, 16:24], sc[:, 32:40], ALU.mult), inc="V9")
                Pn.wait("te", "V9", vv)
                Pn.op("te", TRANS(pGt[0:8, 128:256], wts[:], idF[:]))
                tg2 = Pn.tag("te", "T9")
                Pn.wait("ve", "T9", tg2)
                wtv[i] = Pn.op("ve", CPY(wtT[0:8, i * 128:(i + 1) * 128], pGt[0:8, 128:256]),
                               inc="V9")
            Pn.wait("se", "V9", kb.cnt["V9"])
            Pn.dma("se", dr["sp_wt"][:, :], wtT[:])
            finish_phase(Pn, ph, [("T9", kb.cnt["T9"]), ("V9", kb.cnt["V9"]),
                                  ("S9", kb.cnt["S9"]), ("SQ", kb.cnt["SQ"]),
                                  ("LD", kb.cnt["LD"])])

        for lh in range(2):
            l0 = lh * 1024
            TSM, VSM, SSM = f"T{10 + lh}", f"V{10 + lh}", f"S{10 + lh}"
            with contextlib.ExitStack() as ph:
                Pn = Plan(kb)
                start_phase(Pn)
                xsl = ph.enter_context(_sbuf(nc, "p9xsl", [128, 8 * D], BF16))
                acc = ph.enter_context(_sbuf(nc, "p9acc", [128, 8 * D], F32))
                with contextlib.ExitStack() as ph2:
                    sb = lambda n, s, d: ph2.enter_context(_sbuf(nc, n, list(s), d))
                    hsl = sb("p9hsl", (128, FT * 1024), BF16)
                    w13 = [sb(f"p9w13{j}", (128, DT * 256), BF16) for j in range(2)]
                    w2s = sb("p9w2s", (128, FT * 1024), BF16)
                    srw = sb("p9srw", (1, 1024), F32)
                    srep = sb("p9srep", (128, 1024), F32)
                    tb1 = sb("p9tb1", (128, 1024), BF16)
                    tb2 = sb("p9tb2", (128, 1024), BF16)
                    tb3 = sb("p9tb3", (128, 1024), BF16)
                    pU = ph2.enter_context(_psum(nc, "p9pU", [128, 1024], F32))
                    pV = ph2.enter_context(_psum(nc, "p9pV", [128, 1024], F32))
                    pW = ph2.enter_context(_psum(nc, "p9pW", [128, 1024], F32))
                    pS = ph2.enter_context(_psum(nc, "p9pS", [128, 1024], F32))

                    wx = 0
                    for ltr in range(8):
                        lt = lh * 8 + ltr
                        wx = Pn.dma("sy", A2(xsl, ltr * D, (8 * D, 128), (1, D)),
                                    A2(dr["sp_xn2"], lt * 128 * D, (D, 128), (1, D)), sem="D9")
                    Pn.wait("te", "D9", wx)
                    vsrep_prev = kb.cnt[VSM]
                    prev_tewm = None
                    for ei in range(NE + 1):
                        shared = (ei == 0)
                        e = ei - 1
                        if not shared:
                            if e > 0:
                                Pn.wait("sy", VSM, vsrep_prev)
                            wr = Pn.dma("sy", srw[:], dr["sp_wt"][e:e + 1, l0:l0 + 1024], sem="D8")
                            Pn.wait("te", "D8", wr)
                            Pn.wait("te", VSM, kb.cnt[VSM])
                            Pn.op("te", MM(pS[:, 0:512], onesrow[:], srw[:, 0:512]))
                            Pn.op("te", MM(pS[:, 512:1024], onesrow[:], srw[:, 512:1024]))
                            tm = Pn.tag("te", TSM)
                            Pn.wait("ve", TSM, tm)
                            vsrep_prev = Pn.op("ve", CPY(srep[:], pS[:]), inc=VSM)
                            if _DEBUG and lh == 0:
                                Pn.wait("se", VSM, vsrep_prev)
                                Pn.dma("se", dr["sp_srep"][e:e + 1, :], srep[0:1, :])
                        tewm = {}
                        hv = {}
                        for ht in range(FT):
                            j = ht % 2
                            pA_, pB_ = (pU, pV) if ht % 2 == 0 else (pW, pS)
                            if ht >= 2:
                                Pn.wait("gp", TSM, tewm[ht - 2])
                            elif prev_tewm is not None:
                                Pn.wait("gp", TSM, prev_tewm[ht + FT - 2])
                            if shared:
                                s1 = A3(dr["sw1"], ht * 128, (FHID, 128), (128 * FHID, DT), (1, 128))
                                s3 = A3(dr["sw3"], ht * 128, (FHID, 128), (128 * FHID, DT), (1, 128))
                            else:
                                s1 = A3(dr["w1"], e * D * FHID + ht * 128, (FHID, 128), (128 * FHID, DT), (1, 128))
                                s3 = A3(dr["w3"], e * D * FHID + ht * 128, (FHID, 128), (128 * FHID, DT), (1, 128))
                            Pn.dma("gp", A3(w13[j], 0, (DT * 256, 128), (256, DT), (1, 128)), s1,
                                   sem=f"D{10 + j}")
                            wgq = Pn.dma("gp", A3(w13[j], 128, (DT * 256, 128), (256, DT), (1, 128)), s3,
                                         sem=f"D{10 + j}")
                            Pn.wait("te", f"D{10 + j}", wgq)
                            if ht >= 2:
                                Pn.wait("te", VSM, hv[ht - 2])
                            else:
                                Pn.wait("te", VSM, kb.cnt[VSM])
                            for d in range(DT):
                                for cc in range(2):
                                    rhs = A3(xsl, d * 128 + cc * 4 * D, (8 * D, 128), (D, 4), (1, 128))
                                    Pn.op("te", MM(pA_[:, cc * 512:(cc + 1) * 512],
                                                   w13[j][:, d * 256:d * 256 + 128], rhs,
                                                   start=(d == 0), stop=(d == DT - 1)))
                                    Pn.op("te", MM(pB_[:, cc * 512:(cc + 1) * 512],
                                                   w13[j][:, d * 256 + 128:(d + 1) * 256], rhs,
                                                   start=(d == 0), stop=(d == DT - 1)))
                            tm = Pn.tag("te", TSM)
                            tewm[ht] = tm
                            if shared:
                                Pn.wait("se", TSM, tm)
                                sv = Pn.op("se", ACT(tb1[:], pA_[:], AF.Silu), inc=SSM)
                                Pn.wait("ve", SSM, sv)
                                hv[ht] = Pn.op("ve", TT(hsl[:, ht * 1024:(ht + 1) * 1024],
                                                        tb1[:], pB_[:], ALU.mult), inc=VSM)
                            else:
                                Pn.wait("ve", TSM, tm)
                                Pn.op("ve", TT(tb1[:], pA_[:], srep[:], ALU.mult))
                                vv = Pn.op("ve", TT(tb2[:], pB_[:], srep[:], ALU.mult), inc=VSM)
                                Pn.wait("se", VSM, vv)
                                sv = Pn.op("se", ACT(tb3[:], tb1[:], AF.Silu), inc=SSM)
                                Pn.wait("ve", SSM, sv)
                                hv[ht] = Pn.op("ve", TT(hsl[:, ht * 1024:(ht + 1) * 1024],
                                                        tb3[:], tb2[:], ALU.mult), inc=VSM)
                        vhsl = kb.cnt[VSM]
                        prev_tewm = tewm
                        wv = {}
                        for dh in range(2):
                            if shared:
                                s2 = A3(dr["sw2"], dh * 1024, (D, 128), (128 * D, FT), (1, 1024))
                            else:
                                s2 = A3(dr["w2"], e * FHID * D + dh * 1024, (D, 128), (128 * D, FT), (1, 1024))
                            Pn.wait("gp", TSM, kb.cnt[TSM])
                            w2w = Pn.dma("gp", A3(w2s, 0, (FT * 1024, 128), (1024, FT), (1, 1024)), s2,
                                         sem="D12")
                            Pn.wait("te", "D12", w2w)
                            Pn.wait("te", VSM, vhsl)
                            for ltr in range(8):
                                kk = dh * 8 + ltr
                                pO_ = pU if kk % 2 == 0 else pV
                                if kk >= 2:
                                    Pn.wait("te", VSM, wv[kk - 2])
                                for ht in range(FT):
                                    lhs = hsl[:, ht * 1024 + ltr * 128: ht * 1024 + (ltr + 1) * 128]
                                    Pn.op("te", MM(pO_[:, 0:512], lhs,
                                                   w2s[:, ht * 1024:ht * 1024 + 512],
                                                   start=(ht == 0), stop=(ht == FT - 1)))
                                    Pn.op("te", MM(pO_[:, 512:1024], lhs,
                                                   w2s[:, ht * 1024 + 512:(ht + 1) * 1024],
                                                   start=(ht == 0), stop=(ht == FT - 1)))
                                tm = Pn.tag("te", TSM)
                                Pn.wait("ve", TSM, tm)
                                ac = acc[:, ltr * D + dh * 1024: ltr * D + (dh + 1) * 1024]
                                if shared:
                                    wv[kk] = Pn.op("ve", CPY(ac, pO_[:]), inc=VSM)
                                else:
                                    wv[kk] = Pn.op("ve", TT(ac, ac, pO_[:], ALU.add), inc=VSM)

                with contextlib.ExitStack() as ph3:
                    sb = lambda n, s, d: ph3.enter_context(_sbuf(nc, n, list(s), d))
                    x2t = [sb(f"pEx{j}", (128, D), F32) for j in range(2)]
                    xc = sb("pExc", (128, D), F32)
                    sq = sb("pEsq", (128, D), F32)
                    ot = [sb(f"pEo{j}", (128, D), F32) for j in range(2)]
                    cst = sb("pEcst", (128, 8), F32)
                    vbase = kb.cnt[VSM]
                    spo = {}
                    Pn.wait("sy", VSM, vbase)
                    Pn.wait("sy", TSM, kb.cnt[TSM])
                    Pn.wait("sy", SSM, kb.cnt[SSM])
                    for ltr in range(8):
                        lt = lh * 8 + ltr
                        if ltr >= 2:
                            Pn.wait("sy", VSM, vbase + 4 * (ltr - 2) + 4)
                        wx2 = Pn.dma("sy", x2t[ltr % 2][:], dr["sp_x2"][lt * 128:(lt + 1) * 128, :],
                                     sem=f"D{ltr % 2}")
                        src = acc[:, ltr * D:(ltr + 1) * D]
                        Pn.op("ve", RSUM(cst[:, 0:1], src))
                        Pn.fence("ve")
                        Pn.op("ve", TS(cst[:, 1:2], cst[:, 0:1], 1.0 / D, None, ALU.mult))
                        Pn.fence("ve")
                        vv = Pn.op("ve", TS(xc[:], src, cst[:, 1:2], None, ALU.subtract), inc=VSM)
                        Pn.wait("se", VSM, vv)
                        sv = Pn.op("se", ACT(sq[:], xc[:], AF.Square), inc=SSM)
                        Pn.wait("ve", SSM, sv)
                        vv = Pn.op("ve", RSUM(cst[:, 2:3], sq[:]), inc=VSM)
                        Pn.wait("se", VSM, vv)
                        Pn.op("se", ACT(cst[:, 4:5], cst[:, 2:3], AF.Ln, bias=EPS_LN, scale=1.0 / D))
                        Pn.fence("se")
                        sv = Pn.op("se", ACT(cst[:, 5:6], cst[:, 4:5], AF.Exp, scale=-0.5), inc=SSM)
                        Pn.wait("ve", SSM, sv)
                        vv = Pn.op("ve", TS(xc[:], xc[:], cst[:, 5:6], None, ALU.mult), inc=VSM)
                        Pn.wait("ve", VSM, vv)
                        Pn.wait("ve", f"D{ltr % 2}", wx2)
                        if ltr >= 2:
                            Pn.wait("ve", f"D{19 + ltr % 2}", spo[ltr - 2])
                        vv = Pn.op("ve", TT(ot[ltr % 2][:], x2t[ltr % 2][:], xc[:], ALU.add), inc=VSM)
                        Pn.wait("se", VSM, vv)
                        spo[ltr] = Pn.dma("se", dr["out"][lt * 128:(lt + 1) * 128, :], ot[ltr % 2][:],
                                          sem=f"D{19 + ltr % 2}")
                    finish_phase(Pn, ph, [(TSM, kb.cnt[TSM]), (VSM, kb.cnt[VSM]),
                                          (SSM, kb.cnt[SSM]), ("SQ", kb.cnt["SQ"]),
                                          ("GQ", kb.cnt["GQ"]), ("LD", kb.cnt["LD"])])

        with contextlib.ExitStack() as ph:
            Pn = Plan(kb)
            Pn.wait("gp", "SQ", kb.cnt["SQ"])
            Pn.wait("gp", "D19", kb.cnt["D19"])
            Pn.wait("gp", "D20", kb.cnt["D20"])
            Pn.wait("gp", "PH", kb.cnt["PH"])
            blk = ph.enter_context(nc.Block())
            Pn.run(blk)

    return nc


_CACHE = {}


def _prep_shared(inputs):
    g = lambda n: np.ascontiguousarray(np.asarray(inputs[n], dtype=np.float32))
    o = np.ones((128, 128), np.float32)
    shE0 = np.zeros((128, 128), np.float32)
    shE0[0, 127] = 1.0
    shared = {
        "WxT": np.ascontiguousarray(g("L0_attn_Wx").T),
        "WzT": np.ascontiguousarray(g("L0_attn_Wz").T),
        "WBT": np.ascontiguousarray(g("L0_attn_WB").T),
        "WCT": np.ascontiguousarray(g("L0_attn_WC").T),
        "WthT": np.ascontiguousarray(g("L0_attn_Wth").T),
        "WdlT": np.ascontiguousarray(
            np.concatenate([g("L0_attn_Wdt"), g("L0_attn_Wlam")], axis=0).T),
        "WoutT": np.ascontiguousarray(g("L0_attn_Wout").T),
        "gwT": np.ascontiguousarray(
            g("L0_ffn_gate_w").reshape(16, 128, 8).transpose(1, 0, 2).reshape(128, 128)),
        "w1": np.ascontiguousarray(g("L0_ffn_w1").reshape(NE * D, FHID)),
        "w2": np.ascontiguousarray(g("L0_ffn_w2").reshape(NE * FHID, D)),
        "w3": np.ascontiguousarray(g("L0_ffn_w3").reshape(NE * D, FHID)),
        "sw1": g("L0_ffn_sw1"),
        "sw2": g("L0_ffn_sw2"),
        "sw3": g("L0_ffn_sw3"),
        "Alrep": np.ascontiguousarray(np.tile(g("L0_attn_A_log")[None, :], (128, 1))),
        "ebrep": np.ascontiguousarray(np.tile(g("L0_ffn_expert_bias")[None, :], (128, 1))),
        "bnwB": np.ascontiguousarray(np.tile(g("L0_attn_Bn_w"), (128, 8))),
        "bnwC": np.ascontiguousarray(np.tile(g("L0_attn_Cn_w"), (128, 8))),
        "biasB": np.ascontiguousarray(np.tile(g("L0_attn_B_bias").reshape(1, -1), (128, 1))),
        "biasC": np.ascontiguousarray(np.tile(g("L0_attn_C_bias").reshape(1, -1), (128, 1))),
        "cvals": np.ascontiguousarray(np.tile(np.array([[EPS_LN, 1.0, EPS_RMS, np.pi / 2,
                                                          0.0, 0.0, 0.0, 0.0]], np.float32), (128, 1))),
        "idF": np.eye(128, dtype=np.float32),
        "triL": np.triu(o).astype(np.float32),
        "triUs": np.tril(o, -1).astype(np.float32),
        "maskM": (np.tril(o, -1) * NEG).astype(np.float32),
        "ones128": o,
        "onesrow": np.ones((1, 128), np.float32),
        "onescol": np.ones((128, 1), np.float32),
        "shSup": np.eye(128, k=-1).astype(np.float32),
        "shE0": shE0,
    }
    return shared


def kernel(**inputs):
    x = np.ascontiguousarray(np.asarray(inputs["L0_attn_x"], dtype=np.float32))
    shared = _prep_shared(inputs)
    if "nc" not in _CACHE:
        _CACHE["nc"] = build_nc()
    nc = _CACHE["nc"]

    in_maps = []
    for b in range(8):
        m = dict(shared)
        m["x"] = np.ascontiguousarray(x[b])
        in_maps.append(m)
    res = run_bass_kernel_spmd(nc, in_maps, core_ids=list(range(8)))
    return np.stack([np.asarray(r["out"], dtype=np.float32) for r in res.results], axis=0)


# version 04
# speedup vs baseline: 1.0911x; 1.0911x over previous
import numpy as np
import contextlib
import concourse.bass as bass
import concourse.mybir as mybir
from concourse.bass_utils import run_bass_kernel_spmd

F32 = mybir.dt.float32
BF16 = mybir.dt.bfloat16
AF = mybir.ActivationFunctionType
ALU = mybir.AluOpType
AX = mybir.AxisListType

L = 2048; D = 2048; LT = 16; DT = 16
DIN = 4096; H = 64; NST = 128; PHD = 64; THF = 4096; HNF = 8192
FHID = 1792; FT = 14; NE = 8
PI = float(np.pi); TWO_PI = float(2 * np.pi); MAGIC = 12582912.0
EPS_LN = 1e-5; EPS_RMS = 1e-6; NEG = -1.0e9

_DEBUG = False


class KB:
    def __init__(self, nc, es):
        self.nc = nc
        self.sem = {}
        self.cnt = {}
        for name in ["LD", "GQ", "SQ", "PH"]:
            self._mk(es, name)
        for i in range(22):
            self._mk(es, f"D{i}")
        for nm in ["FV", "FS"]:
            self._mk(es, nm)
        for ph in range(14):
            for pre in ["T", "V", "S"]:
                self._mk(es, f"{pre}{ph}")

    def _mk(self, es, name):
        self.sem[name] = es.enter_context(self.nc.semaphore(name))
        self.cnt[name] = 0


_ALL_PLANS = []


class Plan:
    ENGS = ("sy", "gp", "te", "ve", "se")

    def __init__(self, kb):
        self.kb = kb
        self.s = {e: [] for e in self.ENGS}
        self.ev = {e: [] for e in self.ENGS}
        _ALL_PLANS.append(self)

    def op(self, e, fn, inc=None, n=1):
        if inc is None:
            self.s[e].append(fn)
            self.ev[e].append(("op",))
            return None
        self.kb.cnt[inc] += n
        h = self.kb.sem[inc]
        self.s[e].append(lambda eng, fn=fn, h=h, n=n: fn(eng).then_inc(h, n))
        self.ev[e].append(("inc", inc, n))
        return self.kb.cnt[inc]

    def tag(self, e, inc, n=1):
        self.kb.cnt[inc] += n
        h = self.kb.sem[inc]
        f = self.s[e][-1]
        self.s[e][-1] = lambda eng, f=f, h=h, n=n: f(eng).then_inc(h, n)
        self.ev[e].append(("inc", inc, n))
        return self.kb.cnt[inc]

    def fence(self, e):
        nm = {"ve": "FV", "se": "FS"}[e]
        v = self.tag(e, nm)
        self.wait(e, nm, v)

    def wait(self, e, sname, v):
        if v is None or v <= 0:
            return
        h = self.kb.sem[sname]
        self.s[e].append(lambda eng, h=h, v=v: eng.wait_ge(h, v))
        self.ev[e].append(("wait", sname, v))

    def dma(self, q, dst, src, sem=None):
        cn = sem or {"sy": "LD", "gp": "GQ", "se": "SQ"}[q]
        self.kb.cnt[cn] += 16
        h = self.kb.sem[cn]
        self.s[q].append(lambda eng, d=dst, s=src, h=h: eng.dma_start(d, s).then_inc(h, 16))
        self.ev[q].append(("inc", cn, 16))
        return self.kb.cnt[cn]

    def run(self, blk):
        st = self.s

        @blk.sync
        def _(eng):
            for f in st["sy"]:
                f(eng)

        @blk.gpsimd
        def _(eng):
            for f in st["gp"]:
                f(eng)

        @blk.tensor
        def _(eng):
            for f in st["te"]:
                f(eng)

        @blk.vector
        def _(eng):
            for f in st["ve"]:
                f(eng)

        @blk.scalar
        def _(eng):
            for f in st["se"]:
                f(eng)


def MM(out, lhsT, rhs, start=True, stop=True):
    return lambda e, o=out, l=lhsT, r=rhs, s=start, t=stop: e.matmul(o, l, r, start=s, stop=t)


def TRANS(out, in_, ident):
    return lambda e, o=out, i=in_, d=ident: e.transpose(o, i, d)


def TT(out, a, b, op):
    return lambda e, o=out, x=a, y=b, p=op: e.tensor_tensor(o, x, y, p)


def TS(out, a, s1, s2, op0, op1=None):
    if op1 is None:
        return lambda e, o=out, x=a, u=s1, p=op0: e.tensor_scalar(o, x, u, None, p)
    return lambda e, o=out, x=a, u=s1, w=s2, p=op0, q=op1: e.tensor_scalar(o, x, u, w, p, q)


def STT(out, a, s, b, op0, op1):
    return lambda e, o=out, x=a, u=s, y=b, p=op0, q=op1: e.scalar_tensor_tensor(o, x, u, y, p, q)


def ACT(out, in_, fn, bias=0.0, scale=1.0):
    return lambda e, o=out, i=in_, f=fn, b=bias, s=scale: e.activation(o, i, f, bias=b, scale=s)


def RSUM(out, in_):
    return lambda e, o=out, i=in_: e.reduce_sum(o, i, AX.X)


def RMAX(out, in_):
    return lambda e, o=out, i=in_: e.reduce_max(o, i, AX.X)


def CPY(out, in_):
    return lambda e, o=out, i=in_: e.tensor_copy(o, i)


def SCPY(out, in_):
    return lambda e, o=out, i=in_: e.copy(o, i)


def MST(ap, c):
    return lambda e, a=ap, v=c: e.memset(a, v)


def RECIP(out, in_):
    return lambda e, o=out, i=in_: e.reciprocal(o, i)


def A2(t, off, d0, d1):
    return bass.AP(t, off, [list(d0), list(d1)])


def A3(t, off, d0, d1, d2):
    return bass.AP(t, off, [list(d0), list(d1), list(d2)])



_UID = [0]


def _sbuf(nc, name, shape, dt, side=None):
    _UID[0] += 1
    return nc.sbuf_tensor(f"{name}_{_UID[0]}", list(shape), dt, side=side)


def _psum(nc, name, shape, dt):
    _UID[0] += 1
    return nc.psum_tensor(f"{name}_{_UID[0]}", list(shape), dt)


def build_nc():
    nc = bass.Bass("TRN2", target_bir_lowering=False)
    dr = {}

    def din(name, shape, dt=F32):
        dr[name] = nc.dram_tensor(name, list(shape), dt, kind="ExternalInput")

    def dsc(name, shape, dt):
        kind = "ExternalOutput" if _DEBUG else "Internal"
        dr[name] = nc.dram_tensor(name, list(shape), dt, kind=kind)

    din("x", (L, D))
    din("WxT", (D, DIN)); din("WzT", (D, DIN)); din("WBT", (D, HNF))
    din("WCT", (D, HNF)); din("WthT", (D, THF)); din("WdlT", (D, 128))
    din("WoutT", (DIN, D)); din("gwT", (128, 128))
    din("w1", (NE * D, FHID)); din("w2", (NE * FHID, D)); din("w3", (NE * D, FHID))
    din("sw1", (D, FHID)); din("sw2", (FHID, D)); din("sw3", (D, FHID))
    din("Alrep", (128, H)); din("ebrep", (128, NE))
    din("bnwB", (128, 1024)); din("bnwC", (128, 1024))
    din("biasB", (128, HNF)); din("biasC", (128, HNF))
    din("cvals", (128, 8))
    din("idF", (128, 128)); din("triL", (128, 128)); din("triUs", (128, 128))
    din("maskM", (128, 128)); din("ones128", (128, 128)); din("onesrow", (1, 128))
    din("onescol", (128, 1)); din("shSup", (128, 128)); din("shE0", (128, 128))
    dsc("sp_xp", (L, DIN), F32)
    dsc("sp_B", (L, HNF), BF16)
    dsc("sp_C", (L, HNF), BF16)
    dsc("sp_cos", (L, THF), F32)
    dsc("sp_sin", (L, THF), F32)
    dsc("sp_z", (L, DIN), F32)
    dsc("sp_y", (L, DIN), F32)
    dsc("sp_xn2", (L, D), BF16)
    dsc("sp_o2", (L, D), F32)
    dsc("sp_xn", (L, D), BF16)
    dsc("sp_rr", (L, THF), F32)
    dsc("sp_srep", (NE, 1024), F32)
    dsc("sp_acc", (L, D), F32)
    dsc("sp_x2", (L, D), F32)
    dsc("sp_wt", (NE, L), F32)
    dr["out"] = nc.dram_tensor("out", [L, D], F32, kind="ExternalOutput")

    es_all = contextlib.ExitStack()
    with es_all:
        kb = KB(nc, es_all)
        sball = lambda name, shape, dt: es_all.enter_context(_sbuf(nc, name, list(shape), dt))

        idF = sball("sb_idF", (128, 128), F32)
        idB = sball("sb_idB", (128, 128), BF16)
        triL = sball("sb_triL", (128, 128), F32)
        triUs = sball("sb_triUs", (128, 128), F32)
        maskM = sball("sb_maskM", (128, 128), F32)
        ones128 = sball("sb_ones128", (128, 128), F32)
        onesrow = sball("sb_onesrow", (1, 128), F32)
        onescol = sball("sb_onescol", (128, 1), F32)
        shSup = sball("sb_shSup", (128, 128), F32)
        shE0 = sball("sb_shE0", (128, 128), F32)
        dummy = sball("sb_dummy", (1, 8), F32)
        cvals = sball("sb_cvals", (128, 8), F32)
        for ci, cv in enumerate([EPS_LN, 1.0, EPS_RMS, PI / 2]):
            nc.const_aps.register(F32, float(cv), cvals[:, ci:ci + 1])

        def finish_phase(Pn, scope_es, waiters):
            for (sname, v) in waiters:
                Pn.wait("ve", sname, v)
            for nm in ["LD", "GQ", "SQ"] + [f"D{i}" for i in range(22)]:
                Pn.wait("ve", nm, kb.cnt[nm])
            Pn.op("ve", MST(dummy[:], 0.0), inc="PH")
            blk = scope_es.enter_context(nc.Block())
            Pn.run(blk)

        def start_phase(Pn):
            v = kb.cnt["PH"]
            for e in Plan.ENGS:
                Pn.wait(e, "PH", v)

        proj_es = contextlib.ExitStack()
        xnT = proj_es.enter_context(_sbuf(nc, "sb_xnT", [128, DT * L], BF16, side="right"))

        with contextlib.ExitStack() as ph:
            Pn = Plan(kb)
            sb = lambda n, s, d: ph.enter_context(_sbuf(nc, n, list(s), d))
            pss = lambda n, s, d: ph.enter_context(_psum(nc, n, list(s), d))
            xt = [sb(f"p0xt{j}", (128, D), F32) for j in range(2)]
            xc = sb("p0xc", (128, D), F32)
            sq = sb("p0sq", (128, D), F32)
            xnb = sb("p0xnb", (128, D), BF16)
            cst = sb("p0cst", (128, 8), F32)
            pT = pss("p0pT", (128, 2048), BF16)

            Pn.dma("sy", cvals[:], dr["cvals"][:])
            for t, name in [(idF, "idF"), (triL, "triL"), (triUs, "triUs"),
                            (maskM, "maskM"), (ones128, "ones128"), (onesrow, "onesrow"),
                            (onescol, "onescol"), (shSup, "shSup"), (shE0, "shE0")]:
                Pn.dma("sy", t[:], dr[name][:])
            gq0 = Pn.dma("gp", idB[:], dr["idF"][:])
            ldc = kb.cnt["LD"]
            Pn.wait("se", "LD", ldc)

            for i in range(LT):
                if i >= 2:
                    Pn.wait("sy", "V0", 3 * (i - 2) + 1)
                w = Pn.dma("sy", xt[i % 2][:], dr["x"][i * 128:(i + 1) * 128, :], sem=f"D{i % 2}")
                if i == 0:
                    Pn.wait("ve", "LD", ldc)
                Pn.wait("ve", f"D{i % 2}", w)
                Pn.op("ve", RSUM(cst[:, 0:1], xt[i % 2][:]))
                Pn.fence("ve")
                Pn.op("ve", TS(cst[:, 1:2], cst[:, 0:1], 1.0 / D, None, ALU.mult))
                Pn.fence("ve")
                vv = Pn.op("ve", TS(xc[:], xt[i % 2][:], cst[:, 1:2], None, ALU.subtract), inc="V0")
                Pn.wait("se", "V0", vv)
                sv = Pn.op("se", ACT(sq[:], xc[:], AF.Square), inc="S0")
                Pn.wait("ve", "S0", sv)
                vv = Pn.op("ve", RSUM(cst[:, 2:3], sq[:]), inc="V0")
                Pn.wait("se", "V0", vv)
                Pn.op("se", ACT(cst[:, 4:5], cst[:, 2:3], AF.Ln, bias=EPS_LN, scale=1.0 / D))
                Pn.fence("se")
                sv = Pn.op("se", ACT(cst[:, 5:6], cst[:, 4:5], AF.Exp, scale=-0.5), inc="S0")
                Pn.wait("ve", "S0", sv)
                vv = Pn.op("ve", TS(xnb[:], xc[:], cst[:, 5:6], None, ALU.mult), inc="V0")
                Pn.wait("te", "V0", vv)
                if i == 0:
                    Pn.wait("te", "GQ", gq0)
                else:
                    Pn.wait("te", "S0", 3 * i)
                for k in range(DT):
                    Pn.op("te", TRANS(pT[:, k * 128:(k + 1) * 128],
                                      xnb[:, k * 128:(k + 1) * 128], idB[:]))
                tm = Pn.tag("te", "T0")
                Pn.wait("se", "T0", tm)
                for k in range(DT):
                    Pn.op("se", SCPY(xnT[:, k * L + i * 128: k * L + (i + 1) * 128],
                                     pT[:, k * 128:(k + 1) * 128]))
                sv = Pn.tag("se", "S0")
                if _DEBUG:
                    Pn.wait("se", "S0", sv)
                    Pn.dma("se", dr["sp_xn"][i * 128:(i + 1) * 128, :], xnb[:])
            finish_phase(Pn, ph, [("T0", kb.cnt["T0"]), ("S0", kb.cnt["S0"]),
                                  ("V0", kb.cnt["V0"]), ("LD", kb.cnt["LD"]),
                                  ("GQ", kb.cnt["GQ"])])

        arr_es = contextlib.ExitStack()
        asb = lambda n: arr_es.enter_context(_sbuf(nc, n, [128, LT * H], F32))
        dtf = asb("dtf"); dtA = asb("dtA"); Sar = asb("Sar")
        lgc = asb("lgc"); bpr = asb("bpr"); efl = asb("efl"); gfl = asb("gfl")

        with contextlib.ExitStack() as ph:
            Pn = Plan(kb)
            start_phase(Pn)
            sb = lambda n, s, d: ph.enter_context(_sbuf(nc, n, list(s), d))
            wdl = sb("p1wdl", (128, DT * 128), BF16)
            arep = sb("p1arep", (128, H), F32)
            t0 = sb("p1t0", (128, 128), F32)
            t1 = sb("p1t1", (128, 128), F32)
            t2 = sb("p1t2", (128, 64), F32)
            pA = ph.enter_context(_psum(nc, "p1pA", [128, 512], F32))
            pB = ph.enter_context(_psum(nc, "p1pB", [128, 512], F32))

            wg = Pn.dma("gp", A3(wdl, 0, (DT * 128, 128), (128, DT), (1, 128)),
                        A3(dr["WdlT"], 0, (128, 128), (128 * 128, DT), (1, 128)))
            wl = Pn.dma("sy", arep[:], dr["Alrep"][:])
            Pn.wait("se", "LD", wl)
            sv = Pn.op("se", ACT(arep[:], arep[:], AF.Exp), inc="S1")
            Pn.wait("ve", "S1", sv)
            Pn.op("ve", TS(arep[:], arep[:], -1.0, None, ALU.mult), inc="V1")

            Pn.wait("te", "GQ", wg)
            for i in range(LT):
                if i >= 1:
                    Pn.wait("te", "V1", 1 + 3 * i)
                    Pn.wait("te", "S1", 1 + 2 * i)
                for d in range(DT):
                    Pn.op("te", MM(pA[:, 0:128],
                                   xnT[:, d * L + i * 128: d * L + (i + 1) * 128],
                                   wdl[:, d * 128:(d + 1) * 128],
                                   start=(d == 0), stop=(d == DT - 1)))
                tm = Pn.tag("te", "T1")
                dtc = dtf[:, i * H:(i + 1) * H]
                Pn.wait("ve", "T1", tm)
                Pn.op("ve", TS(t0[:, 0:64], pA[:, 0:64], -1.0, None, ALU.mult))
                Pn.fence("ve")
                vv = Pn.op("ve", TT(t0[:, 0:64], t0[:, 0:64], pA[:, 0:64], ALU.min), inc="V1")
                Pn.wait("se", "V1", vv)
                Pn.op("se", ACT(t1[:, 0:64], t0[:, 0:64], AF.Exp))
                Pn.fence("se")
                sv = Pn.op("se", ACT(t1[:, 0:64], t1[:, 0:64], AF.Ln, bias=1.0), inc="S1")
                Pn.wait("ve", "S1", sv)
                Pn.op("ve", TS(t0[:, 64:128], pA[:, 0:64], 0.0, None, ALU.max))
                Pn.fence("ve")
                vv = Pn.op("ve", TT(dtc, t0[:, 64:128], t1[:, 0:64], ALU.add), inc="V1")
                Pn.wait("se", "V1", vv)
                sv = Pn.op("se", ACT(t1[:, 64:128], pA[:, 64:128], AF.Exp, scale=-1.0), inc="S1")
                Pn.wait("ve", "S1", sv)
                Pn.op("ve", TS(t1[:, 64:128], t1[:, 64:128], 1.0, None, ALU.add))
                Pn.fence("ve")
                Pn.op("ve", RECIP(t2[:, 0:64], t1[:, 64:128]))
                Pn.fence("ve")
                Pn.op("ve", TT(dtA[:, i * H:(i + 1) * H], dtc, arep[:], ALU.mult))
                Pn.op("ve", TT(gfl[:, i * H:(i + 1) * H], t2[:, 0:64], dtc, ALU.mult))
                Pn.fence("ve")
                vv = Pn.op("ve", TT(efl[:, i * H:(i + 1) * H], dtc,
                                    gfl[:, i * H:(i + 1) * H], ALU.subtract), inc="V1")
            for i in range(LT):
                Pn.wait("te", "V1", kb.cnt["V1"])
                Pn.wait("te", "S1", kb.cnt["S1"])
                ei = efl[:, i * H:(i + 1) * H]
                Pn.op("te", MM(pB[:, 0:64], shSup[:], ei, start=True, stop=(i == LT - 1)))
                if i < LT - 1:
                    Pn.op("te", MM(pB[:, 0:64], shE0[:],
                                   efl[:, (i + 1) * H:(i + 2) * H], start=False, stop=True))
                Pn.op("te", MM(pB[:, 64:128], triUs[:], dtA[:, i * H:(i + 1) * H]))
                tm = Pn.tag("te", "T1")
                Pn.wait("ve", "T1", tm)
                Pn.op("ve", CPY(bpr[:, i * H:(i + 1) * H], pB[:, 0:64]))
                vv = Pn.op("ve", TT(t0[:, 0:64], gfl[:, i * H:(i + 1) * H],
                                    pB[:, 0:64], ALU.add), inc="V1")
                Pn.wait("se", "V1", vv)
                Pn.op("se", ACT(lgc[:, i * H:(i + 1) * H], t0[:, 0:64], AF.Ln))
                sv = Pn.op("se", ACT(t1[:, 0:64], pB[:, 64:128], AF.Exp), inc="S1")
                Pn.wait("ve", "S1", sv)
                Pn.op("ve", TT(Sar[:, i * H:(i + 1) * H], t1[:, 0:64],
                               t0[:, 0:64], ALU.mult), inc="V1")
            finish_phase(Pn, ph, [("T1", kb.cnt["T1"]), ("S1", kb.cnt["S1"]),
                                  ("V1", kb.cnt["V1"]), ("LD", kb.cnt["LD"]),
                                  ("GQ", kb.cnt["GQ"])])

        wsl = [proj_es.enter_context(_sbuf(nc, f"sb_wsl{j}", [128, DT * 1024], BF16, side="right"))
               for j in range(2)]

        def slab_src(wname, f0, fw, ocols):
            return A3(dr[wname], f0, (ocols, 128), (128 * ocols, DT), (1, fw))

        def slab_dst(j, fw):
            return A3(wsl[j], 0, (DT * fw, 128), (fw, DT), (1, fw))

        with contextlib.ExitStack() as ph:
            Pn = Plan(kb)
            start_phase(Pn)
            sb = lambda n, s, d: ph.enter_context(_sbuf(nc, n, list(s), d))
            thdt = sb("p2thdt", (128, 1024), F32)
            rr = sb("p2rr", (128, 1024), F32)
            rr2 = sb("p2rr2", (128, 1024), F32)
            carry = sb("p2carry", (1, 1024), F32)
            sst = [sb(f"p2sst{j}", (128, 1024), F32) for j in range(2)]
            cst2 = [sb(f"p2cst{j}", (128, 1024), F32) for j in range(2)]
            pP = [ph.enter_context(_psum(nc, f"p2pP{j}", [128, 1024], F32)) for j in range(2)]
            pQ = ph.enter_context(_psum(nc, "p2pQ", [128, 1024], F32))
            pCr = ph.enter_context(_psum(nc, "p2pCr", [1, 1024], F32))

            NCH = THF // 1024
            slabw = []
            for c in range(NCH):
                if c >= 2:
                    Pn.wait("gp", "T2", 2 * LT * (c - 1))
                slabw.append(Pn.dma("gp", slab_dst(c % 2, 1024), slab_src("WthT", c * 1024, 1024, THF),
                                    sem=f"D{2 + c % 2}"))

            spw = {}
            for c in range(NCH):
                Pn.wait("te", f"D{2 + c % 2}", slabw[c])
                for i in range(LT):
                    k = c * LT + i
                    if k >= 2:
                        Pn.wait("te", "V2", 2 * (k - 1) - 1)
                    for d in range(DT):
                        lhs = xnT[:, d * L + i * 128: d * L + (i + 1) * 128]
                        Pn.op("te", MM(pP[k % 2][:, 0:512], lhs, wsl[c % 2][:, d * 1024:d * 1024 + 512],
                                       start=(d == 0), stop=(d == DT - 1)))
                        Pn.op("te", MM(pP[k % 2][:, 512:1024], lhs, wsl[c % 2][:, d * 1024 + 512:(d + 1) * 1024],
                                       start=(d == 0), stop=(d == DT - 1)))
                    tm = Pn.tag("te", "T2")
                    Pn.wait("ve", "T2", tm)
                    dtb = A3(dtf, i * H + c * 16, (LT * H, 128), (1, 16), (0, 64))
                    vv = Pn.op("ve", TT(thdt[:], pP[k % 2][:], dtb, ALU.mult), inc="V2")
                    Pn.wait("te", "V2", vv)
                    Pn.op("te", MM(pQ[:, 0:512], triL[:], thdt[:, 0:512], start=True, stop=(i == 0)))
                    Pn.op("te", MM(pQ[:, 512:1024], triL[:], thdt[:, 512:1024], start=True, stop=(i == 0)))
                    if i > 0:
                        Pn.op("te", MM(pQ[:, 0:512], onesrow[:], carry[:, 0:512], start=False, stop=True))
                        Pn.op("te", MM(pQ[:, 512:1024], onesrow[:], carry[:, 512:1024], start=False, stop=True))
                    Pn.op("te", MM(pCr[:, 0:512], onescol[:], thdt[:, 0:512], start=True, stop=(i == 0)))
                    Pn.op("te", MM(pCr[:, 512:1024], onescol[:], thdt[:, 512:1024], start=True, stop=(i == 0)))
                    if i > 0:
                        Pn.op("te", MM(pCr[:, 0:512], onesrow[0:1, 0:1], carry[:, 0:512], start=False, stop=True))
                        Pn.op("te", MM(pCr[:, 512:1024], onesrow[0:1, 0:1], carry[:, 512:1024], start=False, stop=True))
                    tm2 = Pn.tag("te", "T2")
                    Pn.wait("ve", "T2", tm2)
                    Pn.op("ve", CPY(carry[:, :], pCr[:, :]))
                    Pn.op("ve", TS(rr[:], pQ[:], 1.0 / TWO_PI, MAGIC, ALU.mult, ALU.add))
                    Pn.fence("ve")
                    Pn.op("ve", TS(rr[:], rr[:], MAGIC, None, ALU.subtract))
                    Pn.fence("ve")
                    Pn.op("ve", STT(rr[:], rr[:], -TWO_PI, pQ[:], ALU.mult, ALU.add))
                    Pn.op("ve", TS(rr2[:], pQ[:], 1.0 / TWO_PI, MAGIC + 0.25, ALU.mult, ALU.add))
                    Pn.fence("ve")
                    Pn.op("ve", TS(rr2[:], rr2[:], MAGIC, None, ALU.subtract))
                    Pn.fence("ve")
                    vv = Pn.op("ve", STT(rr2[:], rr2[:], -TWO_PI, pQ[:], ALU.mult, ALU.add), inc="V2")
                    Pn.wait("se", "V2", vv)
                    if k >= 2:
                        Pn.wait("se", f"D{13 + k % 2}", spw[k - 2])
                    Pn.op("se", ACT(sst[k % 2][:], rr[:], AF.Sin))
                    sv = Pn.op("se", ACT(cst2[k % 2][:], rr2[:], AF.Sin, bias=PI / 2), inc="S2")
                    Pn.wait("se", "S2", sv)
                    if _DEBUG:
                        Pn.dma("se", dr["sp_rr"][i * 128:(i + 1) * 128, c * 1024:(c + 1) * 1024], rr[:])
                        Pn.wait("ve", "SQ", kb.cnt["SQ"])
                    Pn.dma("se", dr["sp_sin"][i * 128:(i + 1) * 128, c * 1024:(c + 1) * 1024],
                           sst[k % 2][:], sem=f"D{13 + k % 2}")
                    spw[k] = Pn.dma("se", dr["sp_cos"][i * 128:(i + 1) * 128, c * 1024:(c + 1) * 1024],
                                    cst2[k % 2][:], sem=f"D{13 + k % 2}")
            finish_phase(Pn, ph, [("T2", kb.cnt["T2"]), ("V2", kb.cnt["V2"]),
                                  ("S2", kb.cnt["S2"]), ("SQ", kb.cnt["SQ"]),
                                  ("GQ", kb.cnt["GQ"])])

        def bc_phase(Wn, bnwn, biasn, spn, TSM, VSM, SSM):
            with contextlib.ExitStack() as ph:
                Pn = Plan(kb)
                start_phase(Pn)
                sb = lambda n, s, d: ph.enter_context(_sbuf(nc, n, list(s), d))
                sq = sb("bcsq", (128, 1024), F32)
                tn = sb("bctn", (128, 1024), F32)
                msc = sb("bcmsc", (128, 24), F32)
                bnw = sb("bcbnw", (128, 1024), F32)
                brep = [sb(f"bcbrep{j}", (128, 1024), F32) for j in range(2)]
                cosb = [sb(f"bccos{j}", (128, 512), F32) for j in range(2)]
                sinb = [sb(f"bcsin{j}", (128, 512), F32) for j in range(2)]
                m1 = sb("bcm1", (128, 512), F32)
                m2 = sb("bcm2", (128, 512), F32)
                stg = [sb(f"bcstg{j}", (128, 1024), BF16) for j in range(2)]
                stgT = [sb(f"bcstgT{j}", (128, 1024), BF16) for j in range(2)]
                pP = [ph.enter_context(_psum(nc, f"bcpP{j}", [128, 1024], F32)) for j in range(2)]
                pTt = [ph.enter_context(_psum(nc, f"bcpTt{j}", [128, 1024], BF16)) for j in range(2)]

                Pn.dma("sy", bnw[:], dr[bnwn][:])
                NCH = HNF // 1024
                slabw, brw, csw = [], [], []
                for c in range(NCH):
                    if c >= 2:
                        Pn.wait("gp", TSM, 2 * LT * (c - 1))
                    slabw.append(Pn.dma("gp", slab_dst(c % 2, 1024), slab_src(Wn, c * 1024, 1024, HNF),
                                        sem=f"D{2 + c % 2}"))
                brw = [Pn.dma("sy", brep[0][:], dr[biasn][:, 0:1024], sem="D4"),
                       Pn.dma("sy", brep[1][:], dr[biasn][:, 1024:2048], sem="D5")]
                for c in range(NCH):
                    h0 = c * 64
                    for i in range(LT):
                        k = c * LT + i
                        if k >= 2:
                            Pn.wait("sy", VSM, 3 * (k - 1))
                        Pn.dma("sy", cosb[k % 2][:], dr["sp_cos"][i * 128:(i + 1) * 128, h0 * 8:h0 * 8 + 512],
                               sem=f"D{6 + k % 2}")
                        csw.append(Pn.dma("sy", sinb[k % 2][:], dr["sp_sin"][i * 128:(i + 1) * 128, h0 * 8:h0 * 8 + 512],
                                          sem=f"D{6 + k % 2}"))
                        if i == LT - 1 and c + 2 < NCH:
                            Pn.wait("sy", VSM, 3 * LT * (c + 1))
                            brw.append(Pn.dma("sy", brep[c % 2][:],
                                              dr[biasn][:, (c + 2) * 1024:(c + 3) * 1024],
                                              sem=f"D{4 + c % 2}"))

                spw = {}
                for c in range(NCH):
                    Pn.wait("te", f"D{2 + c % 2}", slabw[c])
                    for i in range(LT):
                        k = c * LT + i
                        if k >= 2:
                            Pn.wait("te", VSM, 3 * (k - 1))
                        for d in range(DT):
                            lhs = xnT[:, d * L + i * 128: d * L + (i + 1) * 128]
                            Pn.op("te", MM(pP[k % 2][:, 0:512], lhs,
                                           wsl[c % 2][:, d * 1024:d * 1024 + 512],
                                           start=(d == 0), stop=(d == DT - 1)))
                            Pn.op("te", MM(pP[k % 2][:, 512:1024], lhs,
                                           wsl[c % 2][:, d * 1024 + 512:(d + 1) * 1024],
                                           start=(d == 0), stop=(d == DT - 1)))
                        tm = Pn.tag("te", TSM)
                        Pn.wait("se", TSM, tm)
                        sv = Pn.op("se", ACT(sq[:], pP[k % 2][:], AF.Square), inc=SSM)
                        Pn.wait("ve", SSM, sv)
                        for hh in range(8):
                            Pn.op("ve", RSUM(msc[:, hh:hh + 1], sq[:, hh * 128:(hh + 1) * 128]))
                        vv = Pn.tag("ve", VSM)
                        Pn.wait("se", VSM, vv)
                        Pn.op("se", ACT(msc[:, 9:17], msc[:, 0:8], AF.Ln, bias=EPS_RMS, scale=1.0 / NST))
                        Pn.fence("se")
                        sv = Pn.op("se", ACT(msc[:, 9:17], msc[:, 9:17], AF.Exp, scale=-0.5), inc=SSM)
                        Pn.wait("ve", SSM, sv)
                        rstdb = A3(msc, 9, (24, 128), (1, 8), (0, 128))
                        Pn.op("ve", TT(tn[:], pP[k % 2][:], rstdb, ALU.mult))
                        Pn.fence("ve")
                        Pn.op("ve", TT(tn[:], tn[:], bnw[:], ALU.mult))
                        Pn.fence("ve")
                        Pn.wait("ve", f"D{4 + c % 2}", brw[c])
                        vv = Pn.op("ve", TT(tn[:], tn[:], brep[c % 2][:], ALU.add), inc=VSM)
                        Pn.wait("ve", VSM, vv)
                        Pn.wait("ve", f"D{6 + k % 2}", csw[k])
                        ev_in = A2(tn, 0, (1024, 128), (2, 512))
                        od_in = A2(tn, 1, (1024, 128), (2, 512))
                        ev_out = A2(stg[k % 2], 0, (1024, 128), (2, 512))
                        od_out = A2(stg[k % 2], 1, (1024, 128), (2, 512))
                        if k >= 2:
                            Pn.wait("ve", TSM, ttw[k - 2])
                        Pn.op("ve", TT(m1[:], ev_in, cosb[k % 2][:], ALU.mult))
                        Pn.op("ve", TT(m2[:], od_in, sinb[k % 2][:], ALU.mult))
                        Pn.fence("ve")
                        Pn.op("ve", TT(ev_out, m1[:], m2[:], ALU.subtract))
                        Pn.op("ve", TT(m1[:], ev_in, sinb[k % 2][:], ALU.mult))
                        Pn.op("ve", TT(m2[:], od_in, cosb[k % 2][:], ALU.mult))
                        Pn.fence("ve")
                        vv = Pn.op("ve", TT(od_out, m1[:], m2[:], ALU.add), inc=VSM)
                        Pn.wait("te", VSM, vv)
                        if k >= 2:
                            Pn.wait("te", SSM, 3 * (k - 1))
                        for hl in range(8):
                            Pn.op("te", TRANS(pTt[k % 2][:, hl * 128:(hl + 1) * 128],
                                              stg[k % 2][:, hl * 128:(hl + 1) * 128], idB[:]))
                        ttw[k] = Pn.tag("te", TSM)
                        Pn.wait("se", TSM, ttw[k])
                        if k >= 2:
                            Pn.wait("se", f"D{13 + k % 2}", spw[k - 2])
                        for hl in range(8):
                            Pn.op("se", SCPY(stgT[k % 2][:, hl * 128:(hl + 1) * 128],
                                             pTt[k % 2][:, hl * 128:(hl + 1) * 128]))
                        sv = Pn.tag("se", SSM)
                        Pn.wait("se", SSM, sv)
                        spw[k] = Pn.dma("se",
                                        A2(dr[spn], i * 128 * HNF + c * 1024, (HNF, 128), (1, 1024)),
                                        stgT[k % 2][:], sem=f"D{13 + k % 2}")
                finish_phase(Pn, ph, [(TSM, kb.cnt[TSM]), (VSM, kb.cnt[VSM]),
                                      (SSM, kb.cnt[SSM]), ("SQ", kb.cnt["SQ"]),
                                      ("GQ", kb.cnt["GQ"]), ("LD", kb.cnt["LD"])])

        ttw = {}
        bc_phase("WBT", "bnwB", "biasB", "sp_B", "T3", "V3", "S3")
        bc_phase("WCT", "bnwC", "biasC", "sp_C", "T4", "V4", "S4")

        with contextlib.ExitStack() as ph:
            Pn = Plan(kb)
            start_phase(Pn)
            stg = [ph.enter_context(_sbuf(nc, f"p5stg{j}", [128, 1024], F32)) for j in range(2)]
            pP = [ph.enter_context(_psum(nc, f"p5pP{j}", [128, 1024], F32)) for j in range(2)]
            NCH = DIN // 1024
            slabw = []
            for c in range(NCH):
                if c >= 2:
                    Pn.wait("gp", "T5", LT * (c - 1))
                slabw.append(Pn.dma("gp", slab_dst(c % 2, 1024), slab_src("WxT", c * 1024, 1024, DIN),
                                    sem=f"D{2 + c % 2}"))
            spw = {}
            for c in range(NCH):
                Pn.wait("te", f"D{2 + c % 2}", slabw[c])
                for i in range(LT):
                    k = c * LT + i
                    if k >= 2:
                        Pn.wait("te", "S5", k - 1)
                    for d in range(DT):
                        lhs = xnT[:, d * L + i * 128: d * L + (i + 1) * 128]
                        Pn.op("te", MM(pP[k % 2][:, 0:512], lhs, wsl[c % 2][:, d * 1024:d * 1024 + 512],
                                       start=(d == 0), stop=(d == DT - 1)))
                        Pn.op("te", MM(pP[k % 2][:, 512:1024], lhs, wsl[c % 2][:, d * 1024 + 512:(d + 1) * 1024],
                                       start=(d == 0), stop=(d == DT - 1)))
                    tm = Pn.tag("te", "T5")
                    Pn.wait("se", "T5", tm)
                    if k >= 2:
                        Pn.wait("se", f"D{13 + k % 2}", spw[k - 2])
                    sv = Pn.op("se", SCPY(stg[k % 2][:], pP[k % 2][:]), inc="S5")
                    Pn.wait("se", "S5", sv)
                    spw[k] = Pn.dma("se", dr["sp_xp"][i * 128:(i + 1) * 128, c * 1024:(c + 1) * 1024],
                                    stg[k % 2][:], sem=f"D{13 + k % 2}")
            finish_phase(Pn, ph, [("T5", kb.cnt["T5"]), ("S5", kb.cnt["S5"]),
                                  ("SQ", kb.cnt["SQ"]), ("GQ", kb.cnt["GQ"])])

        with contextlib.ExitStack() as ph:
            Pn = Plan(kb)
            start_phase(Pn)
            zsl = [ph.enter_context(_sbuf(nc, f"p6zsl{j}", [128, DT * 128], BF16)) for j in range(2)]
            stg = [ph.enter_context(_sbuf(nc, f"p6stg{j}", [128, 512], F32)) for j in range(2)]
            pZ = [ph.enter_context(_psum(nc, f"p6pZ{j}", [128, 512], F32)) for j in range(2)]
            NFT = DIN // 128
            slabw = []
            for ft in range(NFT):
                if ft >= 2:
                    Pn.wait("gp", "T6", 4 * (ft - 1))
                slabw.append(Pn.dma("gp",
                                    A3(zsl[ft % 2], 0, (DT * 128, 128), (128, DT), (1, 128)),
                                    A3(dr["WzT"], ft * 128, (DIN, 128), (128 * DIN, DT), (1, 128)),
                                    sem=f"D{2 + ft % 2}"))
            spw = {}
            for ft in range(NFT):
                Pn.wait("te", f"D{2 + ft % 2}", slabw[ft])
                for lc in range(4):
                    k = ft * 4 + lc
                    if k >= 2:
                        Pn.wait("te", "S6", k - 1)
                    for d in range(DT):
                        Pn.op("te", MM(pZ[k % 2][:], zsl[ft % 2][:, d * 128:(d + 1) * 128],
                                       xnT[:, d * L + lc * 512:d * L + (lc + 1) * 512],
                                       start=(d == 0), stop=(d == DT - 1)))
                    tm = Pn.tag("te", "T6")
                    Pn.wait("se", "T6", tm)
                    if k >= 2:
                        Pn.wait("se", f"D{13 + k % 2}", spw[k - 2])
                    sv = Pn.op("se", SCPY(stg[k % 2][:], pZ[k % 2][:]), inc="S6")
                    Pn.wait("se", "S6", sv)
                    for b in range(4):
                        lt = lc * 4 + b
                        spw[k] = Pn.dma("se",
                                        A2(dr["sp_z"], lt * 128 * DIN + ft * 128, (DIN, 128), (1, 128)),
                                        A2(stg[k % 2], b * 128, (512, 128), (1, 128)),
                                        sem=f"D{13 + k % 2}")
            finish_phase(Pn, ph, [("T6", kb.cnt["T6"]), ("S6", kb.cnt["S6"]),
                                  ("SQ", kb.cnt["SQ"]), ("GQ", kb.cnt["GQ"])])

        proj_es.close()

        with contextlib.ExitStack() as ph:
            Pn = Plan(kb)
            start_phase(Pn)
            sb = lambda n, s, d: ph.enter_context(_sbuf(nc, n, list(s), d))
            Bt = [sb(f"scB{j}", (128, HNF), BF16) for j in range(2)]
            Ct = [sb(f"scC{j}", (128, HNF), BF16) for j in range(2)]
            Xt = [sb(f"scX{j}", (128, DIN), F32) for j in range(2)]
            Hf = sb("scHf", (128, H * PHD), F32)
            yst = sb("scyst", (128, DIN), F32)
            V04 = [sb(f"scV0{j}", (128, 512), F32) for j in range(2)]
            Db4 = [sb(f"scDb{j}", (128, 512), F32) for j in range(2)]
            E4 = [sb(f"scE4{j}", (128, 512), F32) for j in range(2)]
            tI4 = [sb(f"scI4{j}", (128, 512), F32) for j in range(2)]
            R4 = [sb(f"scR4{j}", (128, 512), F32) for j in range(2)]
            KT = [sb(f"scKT{j}", (128, 128), F32) for j in range(2)]
            Wsb = [sb(f"scW{j}", (128, 128), F32) for j in range(2)]
            Cti = [sb(f"scCt{j}", (128, 128), F32) for j in range(2)]
            Htm = [sb(f"scHt{j}", (128, PHD), F32) for j in range(2)]
            pDR = [ph.enter_context(_psum(nc, f"scpDR{j}", [128, 1024], F32)) for j in range(2)]
            pGY = [ph.enter_context(_psum(nc, f"scpGY{j}", [128, 512], F32)) for j in range(2)]
            pTK = [ph.enter_context(_psum(nc, f"scpTK{j}", [128, 1024], BF16)) for j in range(2)]

            NQ = LT
            v_init = Pn.op("ve", MST(Hf[:], 0.0), inc="V7")
            tgte, tgv0, tgve, tgse = {}, {}, {}, {}
            th1, th2, hv, hs1, hs2 = {}, {}, {}, {}, {}

            ldw = []
            qdone = {}

            def plan_load(q):
                if q >= 2:
                    Pn.wait("sy", "T7", qdone[q - 2])
                Pn.dma("sy", Bt[q % 2][:], dr["sp_B"][q * 128:(q + 1) * 128, :], sem=f"D{q % 2}")
                Pn.dma("sy", Ct[q % 2][:], dr["sp_C"][q * 128:(q + 1) * 128, :], sem=f"D{q % 2}")
                ldw.append(Pn.dma("sy", Xt[q % 2][:], dr["sp_xp"][q * 128:(q + 1) * 128, :], sem=f"D{q % 2}"))

            spillw = [0] * NQ
            plan_load(0)
            plan_load(1)
            for q in range(NQ):
                qp = q % 2
                for g in range(16):
                    gi = q * 16 + g
                    gp2 = gi % 2
                    cb = q * H + g * 4
                    trilb = A3(triL, 0, (128, 128), (0, 4), (1, 128))
                    maskb = A3(maskM, 0, (128, 128), (0, 4), (1, 128))
                    idFb = A3(idF, 0, (128, 128), (0, 4), (1, 128))
                    dtAb = A3(dtA, cb, (LT * H, 128), (1, 4), (0, 128))
                    lgcb = A3(lgc, cb, (LT * H, 128), (1, 4), (0, 128))
                    bprb = A3(bpr, cb, (LT * H, 128), (1, 4), (0, 128))
                    if gi == 0:
                        Pn.wait("ve", "D0", ldw[0])
                    if g == 0 and q > 0:
                        Pn.wait("ve", f"D{qp}", ldw[q])
                    Pn.wait("ve", "T7", tgte.get(gi - 2, 0))
                    tgv0[gi] = Pn.op("ve", TT(V04[gp2][:], trilb, dtAb, ALU.mult), inc="V7")
                    if gi == 0:
                        Pn.wait("te", "D0", ldw[0])
                        Pn.wait("te", "V7", v_init)
                    if g == 0 and q > 0:
                        Pn.wait("te", f"D{qp}", ldw[q])
                    Pn.wait("te", "V7", tgv0[gi])
                    Pn.wait("te", "V7", tgve.get(gi - 2, 0))
                    Pn.wait("te", "S7", tgse.get(gi - 2, 0))
                    Pn.op("te", MM(pDR[gp2][:, 0:512], triUs[:], V04[gp2][:]))
                    Pn.op("te", MM(pDR[gp2][:, 512:1024], ones128[:], V04[gp2][:]))
                    tgte[gi] = Pn.tag("te", "T7")
                    Pn.wait("ve", "T7", tgte[gi])
                    Pn.op("ve", TT(Db4[gp2][:], pDR[gp2][:, 0:512], maskb, ALU.add))
                    Pn.fence("ve")
                    vv = Pn.op("ve", TT(Db4[gp2][:], Db4[gp2][:], lgcb, ALU.add), inc="V7")
                    Pn.wait("se", "V7", vv)
                    Pn.op("se", ACT(E4[gp2][:], Db4[gp2][:], AF.Exp))
                    Pn.op("se", ACT(R4[gp2][:], pDR[gp2][:, 512:1024], AF.Exp))
                    tgse[gi] = Pn.tag("se", "S7")
                    Pn.op("ve", TT(tI4[gp2][:], idFb, bprb, ALU.mult))
                    Pn.wait("ve", "S7", tgse[gi])
                    Pn.fence("ve")
                    vv = Pn.op("ve", TT(E4[gp2][:], E4[gp2][:], tI4[gp2][:], ALU.subtract), inc="V7")
                    tgve[gi] = vv
                    Pn.wait("ve", "V7", vv)
                    for hh in range(4):
                        h = g * 4 + hh
                        idx = q * H + h
                        p = idx % 2
                        col = q * H + h
                        Ksl = Bt[qp][:, h * 128:(h + 1) * 128]
                        Csl = Ct[qp][:, h * 128:(h + 1) * 128]
                        Xsl = Xt[qp][:, h * 64:(h + 1) * 64]
                        Pn.wait("te", "V7", hv.get(idx - 2, 0))
                        Pn.wait("te", "S7", hs2.get(idx - 2, 0))
                        Pn.op("te", MM(pGY[p][:, 0:128], Ksl, Csl))
                        Pn.op("te", TRANS(pTK[p][:, 0:128], Ksl, idB[:]))
                        th1[idx] = Pn.tag("te", "T7")
                        Pn.wait("se", "T7", th1[idx])
                        Pn.op("se", (lambda e, o=KT[p][:], i_=pTK[p][:, 0:128], sc=Sar[:, col:col + 1]:
                                     e.activation(o, i_, AF.Copy, scale=sc)))
                        hs1[idx] = Pn.tag("se", "S7")
                        Pn.wait("ve", "T7", th1[idx])
                        Pn.op("ve", TT(Wsb[p][:], E4[gp2][:, hh * 128:(hh + 1) * 128],
                                       pGY[p][:, 0:128], ALU.mult), inc="V7")
                        vv = Pn.op("ve", TT(Cti[p][:], Csl, R4[gp2][:, hh * 128:(hh + 1) * 128],
                                            ALU.mult), inc="V7")
                        Pn.wait("te", "V7", vv)
                        Pn.wait("te", "S7", hs1[idx])
                        Pn.op("te", MM(pGY[p][0:64, 128:256], Xsl, Wsb[p][:], start=True, stop=False))
                        Pn.op("te", MM(pGY[p][0:64, 128:256], Hf[:, h * PHD:(h + 1) * PHD],
                                       Cti[p][:], start=False, stop=True))
                        Pn.op("te", MM(pGY[p][:, 256:320], KT[p][:], Xsl))
                        th2[idx] = Pn.tag("te", "T7")
                        Pn.wait("ve", "T7", th2[idx])
                        Pn.op("ve", TS(Htm[p][:], Hf[:, h * PHD:(h + 1) * PHD],
                                       R4[gp2][:, hh * 128 + 127:hh * 128 + 128], None, ALU.mult))
                        Pn.fence("ve")
                        hv[idx] = Pn.op("ve", TT(Hf[:, h * PHD:(h + 1) * PHD], Htm[p][:],
                                                 pGY[p][:, 256:320], ALU.add), inc="V7")
                        Pn.wait("se", "T7", th2[idx])
                        if h == 0 and q >= 1:
                            Pn.wait("se", "D15", spillw[q - 1])
                        Pn.op("se", SCPY(yst[p * 64:(p + 1) * 64, (h // 2) * 128:(h // 2 + 1) * 128],
                                         pGY[p][0:64, 128:256]))
                        hs2[idx] = Pn.tag("se", "S7")
                qdone[q] = th2[q * H + H - 1]
                if q + 2 < NQ:
                    plan_load(q + 2)
                Pn.wait("se", "S7", hs2[q * H + H - 1])
                spillw[q] = Pn.dma("se", A2(dr["sp_y"], q * 128 * DIN, (DIN, 128), (1, DIN)),
                                   yst[:], sem="D15")
            finish_phase(Pn, ph, [("T7", kb.cnt["T7"]), ("V7", kb.cnt["V7"]),
                                  ("S7", kb.cnt["S7"]), ("SQ", kb.cnt["SQ"]),
                                  ("LD", kb.cnt["LD"])])

        arr_es.close()

        with contextlib.ExitStack() as ph:
            Pn = Plan(kb)
            start_phase(Pn)
            sb = lambda n, s, d: ph.enter_context(_sbuf(nc, n, list(s), d))
            wsl8 = sb("p8wsl", (128, 32 * 1024), BF16)
            ytb = [sb(f"p8y{j}", (128, DIN), F32) for j in range(2)]
            ztb = [sb(f"p8z{j}", (128, DIN), F32) for j in range(2)]
            zs = sb("p8zs", (128, DIN), BF16)
            vst = [sb(f"p8v{j}", (128, DIN), BF16) for j in range(2)]
            o2s = [sb(f"p8o{j}", (128, 1024), F32) for j in range(2)]
            pO = [ph.enter_context(_psum(nc, f"p8pO{j}", [128, 1024], F32)) for j in range(2)]

            base_T = kb.cnt["T8"]
            spw = {}
            for dh in range(2):
                if dh == 1:
                    Pn.wait("gp", "T8", base_T + LT)
                wq = Pn.dma("gp", A3(wsl8, 0, (32 * 1024, 128), (1024, 32), (1, 1024)),
                            A3(dr["WoutT"], dh * 1024, (D, 128), (128 * D, 32), (1, 1024)))
                ldy = []
                for lt in range(LT):
                    k = dh * LT + lt
                    if k >= 2:
                        Pn.wait("sy", "V8", 2 * (k - 1) - 1)
                    Pn.dma("sy", ytb[lt % 2][:], dr["sp_y"][lt * 128:(lt + 1) * 128, :], sem=f"D{k % 2}")
                    ldy.append(Pn.dma("sy", ztb[lt % 2][:], dr["sp_z"][lt * 128:(lt + 1) * 128, :],
                                      sem=f"D{k % 2}"))
                for lt in range(LT):
                    k = dh * LT + lt
                    Pn.wait("se", f"D{k % 2}", ldy[lt])
                    if k >= 1:
                        Pn.wait("se", "V8", 2 * (k - 1) + 1)
                    sv = Pn.op("se", ACT(zs[:], ztb[lt % 2][:], AF.Silu), inc="S8")
                    Pn.wait("ve", "S8", sv)
                    if k >= 2:
                        Pn.wait("ve", "T8", base_T + k - 1)
                    vv = Pn.op("ve", TT(vst[lt % 2][:], ytb[lt % 2][:], zs[:], ALU.mult), inc="V8")
                    Pn.wait("te", "V8", vv)
                    if lt == 0:
                        Pn.wait("te", "GQ", wq)
                    if k >= 2:
                        Pn.wait("te", "V8", 2 * (k - 1))
                    for d2 in range(32):
                        lhs = vst[lt % 2][:, d2 * 128:(d2 + 1) * 128]
                        Pn.op("te", MM(pO[k % 2][:, 0:512], lhs, wsl8[:, d2 * 1024:d2 * 1024 + 512],
                                       start=(d2 == 0), stop=(d2 == 31)))
                        Pn.op("te", MM(pO[k % 2][:, 512:1024], lhs, wsl8[:, d2 * 1024 + 512:(d2 + 1) * 1024],
                                       start=(d2 == 0), stop=(d2 == 31)))
                    tm = Pn.tag("te", "T8")
                    Pn.wait("ve", "T8", tm)
                    if k >= 2:
                        Pn.wait("ve", f"D{13 + k % 2}", spw[k - 2])
                    vv = Pn.op("ve", CPY(o2s[k % 2][:], pO[k % 2][:]), inc="V8")
                    Pn.wait("se", "V8", vv)
                    spw[k] = Pn.dma("se", dr["sp_o2"][lt * 128:(lt + 1) * 128, dh * 1024:(dh + 1) * 1024],
                                    o2s[k % 2][:], sem=f"D{13 + k % 2}")
            finish_phase(Pn, ph, [("T8", kb.cnt["T8"]), ("V8", kb.cnt["V8"]),
                                  ("S8", kb.cnt["S8"]), ("SQ", kb.cnt["SQ"]),
                                  ("GQ", kb.cnt["GQ"]), ("LD", kb.cnt["LD"])])

        with contextlib.ExitStack() as ph:
            Pn = Plan(kb)
            start_phase(Pn)
            sb = lambda n, s, d: ph.enter_context(_sbuf(nc, n, list(s), d))
            o2t = [sb(f"p8bo{j}", (128, D), F32) for j in range(2)]
            xt = [sb(f"p8bx{j}", (128, D), F32) for j in range(2)]
            xc = sb("p8bxc", (128, D), F32)
            sq = sb("p8bsq", (128, D), F32)
            x2t = sb("p8bx2", (128, D), F32)
            xnb = sb("p8bxnb", (128, D), BF16)
            xn2f = sb("p8bxn2f", (128, D), F32)
            xfst = sb("p8bxfst", (128, D), F32)
            stg = sb("p8bstg", (128, D), BF16)
            cst = sb("p8bcst", (128, 8), F32)
            gwF = sb("p8bgw", (128, 128), F32)
            ebr = sb("p8bebr", (128, 8), F32)
            sc = sb("p8bsc", (128, 48), F32)
            wts = sb("p8bwts", (128, 8), F32)
            wtT = sb("p8bwtT", (8, L), F32)
            pT = ph.enter_context(_psum(nc, "p8bpT", [128, 2048], BF16))
            pTf = ph.enter_context(_psum(nc, "p8bpTf", [128, 2048], F32))
            pGt = ph.enter_context(_psum(nc, "p8bpGt", [128, 512], F32))
            wgf = Pn.dma("sy", gwF[:], dr["gwT"][:])
            wef = Pn.dma("sy", ebr[:], dr["ebrep"][:])

            def ln_into(src_ap, out_ap, out_is_bf):
                Pn.op("ve", RSUM(cst[:, 0:1], src_ap))
                Pn.fence("ve")
                Pn.op("ve", TS(cst[:, 1:2], cst[:, 0:1], 1.0 / D, None, ALU.mult))
                Pn.fence("ve")
                vv = Pn.op("ve", TS(xc[:], src_ap, cst[:, 1:2], None, ALU.subtract), inc="V9")
                Pn.wait("se", "V9", vv)
                sv = Pn.op("se", ACT(sq[:], xc[:], AF.Square), inc="S9")
                Pn.wait("ve", "S9", sv)
                vv = Pn.op("ve", RSUM(cst[:, 2:3], sq[:]), inc="V9")
                Pn.wait("se", "V9", vv)
                Pn.op("se", ACT(cst[:, 4:5], cst[:, 2:3], AF.Ln, bias=EPS_LN, scale=1.0 / D))
                Pn.fence("se")
                sv = Pn.op("se", ACT(cst[:, 5:6], cst[:, 4:5], AF.Exp, scale=-0.5), inc="S9")
                Pn.wait("ve", "S9", sv)
                return Pn.op("ve", TS(out_ap, xc[:], cst[:, 5:6], None, ALU.mult), inc="V9")

            spx = {}
            spn = {}
            wtv = {}
            ve_tile = 14
            se_tile = 7
            for i in range(LT):
                if i >= 2:
                    Pn.wait("sy", "V9", ve_tile * (i - 2) + 4)
                Pn.dma("sy", o2t[i % 2][:], dr["sp_o2"][i * 128:(i + 1) * 128, :], sem=f"D{i % 2}")
                w2_ = Pn.dma("sy", xt[i % 2][:], dr["x"][i * 128:(i + 1) * 128, :], sem=f"D{i % 2}")
                Pn.wait("ve", f"D{i % 2}", w2_)
                lnv = ln_into(o2t[i % 2][:], xc[:], False)
                if i >= 1:
                    Pn.wait("ve", "D17", spx[i - 1])
                Pn.wait("ve", "V9", lnv)
                vv = Pn.op("ve", TT(x2t[:], xt[i % 2][:], xc[:], ALU.add), inc="V9")
                Pn.wait("se", "V9", vv)
                spx[i] = Pn.dma("se", dr["sp_x2"][i * 128:(i + 1) * 128, :], x2t[:], sem="D17")
                vv = ln_into(x2t[:], xn2f[:], False)
                Pn.wait("ve", "V9", vv)
                vv = Pn.op("ve", CPY(xnb[:], xn2f[:]), inc="V9")
                Pn.wait("te", "V9", vv)
                if i >= 1:
                    Pn.wait("te", "S9", se_tile * i)
                for k in range(DT):
                    Pn.op("te", TRANS(pT[:, k * 128:(k + 1) * 128],
                                      xnb[:, k * 128:(k + 1) * 128], idB[:]))
                tm = Pn.tag("te", "T9")
                for k in range(DT):
                    Pn.op("te", TRANS(pTf[:, k * 128:(k + 1) * 128],
                                      xn2f[:, k * 128:(k + 1) * 128], idF[:]))
                tmf = Pn.tag("te", "T9")
                Pn.wait("se", "T9", tm)
                if i >= 1:
                    Pn.wait("se", "D18", spn[i - 1])
                for k in range(DT):
                    Pn.op("se", SCPY(stg[:, k * 128:(k + 1) * 128], pT[:, k * 128:(k + 1) * 128]))
                sv = Pn.tag("se", "S9")
                Pn.wait("se", "S9", sv)
                spn[i] = Pn.dma("se", dr["sp_xn2"][i * 128:(i + 1) * 128, :], stg[:], sem="D18")
                Pn.wait("se", "T9", tmf)
                for k in range(DT):
                    Pn.op("se", SCPY(xfst[:, k * 128:(k + 1) * 128], pTf[:, k * 128:(k + 1) * 128]))
                sv = Pn.tag("se", "S9")
                Pn.wait("te", "S9", sv)
                Pn.wait("te", "LD", wgf)
                if i >= 1:
                    Pn.wait("te", "V9", wtv[i - 1])
                for d in range(DT):
                    Pn.op("te", MM(pGt[:, 0:8], xfst[:, d * 128:(d + 1) * 128],
                                   gwF[:, d * 8:(d + 1) * 8],
                                   start=(d == 0), stop=(d == DT - 1)))
                tg = Pn.tag("te", "T9")
                Pn.wait("ve", "LD", wef)
                Pn.wait("ve", "T9", tg)
                Pn.op("ve", RMAX(sc[:, 0:1], pGt[:, 0:8]))
                Pn.fence("ve")
                vv = Pn.op("ve", TS(sc[:, 1:2], sc[:, 0:1], -1.0, None, ALU.mult), inc="V9")
                Pn.wait("se", "V9", vv)
                sv = Pn.op("se", ACT(sc[:, 8:16], pGt[:, 0:8], AF.Exp, bias=sc[:, 1:2]), inc="S9")
                Pn.wait("ve", "S9", sv)
                Pn.op("ve", RSUM(sc[:, 2:3], sc[:, 8:16]))
                Pn.fence("ve")
                Pn.op("ve", RECIP(sc[:, 3:4], sc[:, 2:3]))
                Pn.fence("ve")
                vv = Pn.op("ve", TS(sc[:, 16:24], sc[:, 8:16], sc[:, 3:4], None, ALU.mult), inc="V9")
                Pn.wait("ve", "V9", vv)
                Pn.op("ve", TT(sc[:, 24:32], sc[:, 16:24], ebr[:], ALU.add))
                Pn.fence("ve")
                Pn.op("ve", RMAX(sc[:, 4:5], sc[:, 24:32]))
                Pn.fence("ve")
                vv = Pn.op("ve", TS(sc[:, 32:40], sc[:, 24:32], sc[:, 4:5], None, ALU.is_ge), inc="V9")
                Pn.wait("ve", "V9", vv)
                Pn.op("ve", STT(sc[:, 40:48], sc[:, 32:40], NEG, sc[:, 24:32], ALU.mult, ALU.add))
                Pn.fence("ve")
                Pn.op("ve", RMAX(sc[:, 5:6], sc[:, 40:48]))
                Pn.fence("ve")
                vv = Pn.op("ve", TS(sc[:, 32:40], sc[:, 24:32], sc[:, 5:6], None, ALU.is_ge), inc="V9")
                Pn.wait("ve", "V9", vv)
                vv = Pn.op("ve", TT(wts[:], sc[:, 16:24], sc[:, 32:40], ALU.mult), inc="V9")
                Pn.wait("te", "V9", vv)
                Pn.op("te", TRANS(pGt[0:8, 128:256], wts[:], idF[:]))
                tg2 = Pn.tag("te", "T9")
                Pn.wait("ve", "T9", tg2)
                wtv[i] = Pn.op("ve", CPY(wtT[0:8, i * 128:(i + 1) * 128], pGt[0:8, 128:256]),
                               inc="V9")
            Pn.wait("se", "V9", kb.cnt["V9"])
            Pn.dma("se", dr["sp_wt"][:, :], wtT[:])
            finish_phase(Pn, ph, [("T9", kb.cnt["T9"]), ("V9", kb.cnt["V9"]),
                                  ("S9", kb.cnt["S9"]), ("SQ", kb.cnt["SQ"]),
                                  ("LD", kb.cnt["LD"])])

        for lh in range(2):
            l0 = lh * 1024
            TSM, VSM, SSM = f"T{10 + lh}", f"V{10 + lh}", f"S{10 + lh}"
            with contextlib.ExitStack() as ph:
                Pn = Plan(kb)
                start_phase(Pn)
                xsl = ph.enter_context(_sbuf(nc, "p9xsl", [128, 8 * D], BF16))
                acc = ph.enter_context(_sbuf(nc, "p9acc", [128, 8 * D], F32))
                with contextlib.ExitStack() as ph2:
                    sb = lambda n, s, d: ph2.enter_context(_sbuf(nc, n, list(s), d))
                    hsl = sb("p9hsl", (128, FT * 1024), BF16)
                    w13 = [sb(f"p9w13{j}", (128, DT * 256), BF16) for j in range(2)]
                    w2s = sb("p9w2s", (128, FT * 1024), BF16)
                    srw = sb("p9srw", (1, 1024), F32)
                    srep = sb("p9srep", (128, 1024), F32)
                    tb1 = sb("p9tb1", (128, 1024), BF16)
                    tb2 = sb("p9tb2", (128, 1024), BF16)
                    tb3 = sb("p9tb3", (128, 1024), BF16)
                    pU = ph2.enter_context(_psum(nc, "p9pU", [128, 1024], F32))
                    pV = ph2.enter_context(_psum(nc, "p9pV", [128, 1024], F32))
                    pW = ph2.enter_context(_psum(nc, "p9pW", [128, 1024], F32))
                    pS = ph2.enter_context(_psum(nc, "p9pS", [128, 1024], F32))

                    wx = 0
                    for ltr in range(8):
                        lt = lh * 8 + ltr
                        wx = Pn.dma("sy", A2(xsl, ltr * D, (8 * D, 128), (1, D)),
                                    A2(dr["sp_xn2"], lt * 128 * D, (D, 128), (1, D)), sem="D9")
                    Pn.wait("te", "D9", wx)
                    vsrep_prev = kb.cnt[VSM]
                    prev_tewm = None
                    for ei in range(NE + 1):
                        shared = (ei == 0)
                        e = ei - 1
                        if not shared:
                            if e > 0:
                                Pn.wait("sy", VSM, vsrep_prev)
                            wr = Pn.dma("sy", srw[:], dr["sp_wt"][e:e + 1, l0:l0 + 1024], sem="D8")
                            Pn.wait("te", "D8", wr)
                            Pn.wait("te", VSM, kb.cnt[VSM])
                            Pn.op("te", MM(pS[:, 0:512], onesrow[:], srw[:, 0:512]))
                            Pn.op("te", MM(pS[:, 512:1024], onesrow[:], srw[:, 512:1024]))
                            tm = Pn.tag("te", TSM)
                            Pn.wait("ve", TSM, tm)
                            vsrep_prev = Pn.op("ve", CPY(srep[:], pS[:]), inc=VSM)
                            if _DEBUG and lh == 0:
                                Pn.wait("se", VSM, vsrep_prev)
                                Pn.dma("se", dr["sp_srep"][e:e + 1, :], srep[0:1, :])
                        tewm = {}
                        hv = {}
                        for ht in range(FT):
                            j = ht % 2
                            pA_, pB_ = (pU, pV) if ht % 2 == 0 else (pW, pS)
                            if ht >= 2:
                                Pn.wait("gp", TSM, tewm[ht - 2])
                            elif prev_tewm is not None:
                                Pn.wait("gp", TSM, prev_tewm[ht + FT - 2])
                            if shared:
                                s1 = A3(dr["sw1"], ht * 128, (FHID, 128), (128 * FHID, DT), (1, 128))
                                s3 = A3(dr["sw3"], ht * 128, (FHID, 128), (128 * FHID, DT), (1, 128))
                            else:
                                s1 = A3(dr["w1"], e * D * FHID + ht * 128, (FHID, 128), (128 * FHID, DT), (1, 128))
                                s3 = A3(dr["w3"], e * D * FHID + ht * 128, (FHID, 128), (128 * FHID, DT), (1, 128))
                            Pn.dma("gp", A3(w13[j], 0, (DT * 256, 128), (256, DT), (1, 128)), s1,
                                   sem=f"D{10 + j}")
                            wgq = Pn.dma("gp", A3(w13[j], 128, (DT * 256, 128), (256, DT), (1, 128)), s3,
                                         sem=f"D{10 + j}")
                            Pn.wait("te", f"D{10 + j}", wgq)
                            if ht >= 2:
                                Pn.wait("te", VSM, hv[ht - 2])
                            else:
                                Pn.wait("te", VSM, kb.cnt[VSM])
                            for d in range(DT):
                                for cc in range(2):
                                    rhs = A3(xsl, d * 128 + cc * 4 * D, (8 * D, 128), (D, 4), (1, 128))
                                    Pn.op("te", MM(pA_[:, cc * 512:(cc + 1) * 512],
                                                   w13[j][:, d * 256:d * 256 + 128], rhs,
                                                   start=(d == 0), stop=(d == DT - 1)))
                                    Pn.op("te", MM(pB_[:, cc * 512:(cc + 1) * 512],
                                                   w13[j][:, d * 256 + 128:(d + 1) * 256], rhs,
                                                   start=(d == 0), stop=(d == DT - 1)))
                            tm = Pn.tag("te", TSM)
                            tewm[ht] = tm
                            if shared:
                                Pn.wait("se", TSM, tm)
                                sv = Pn.op("se", ACT(tb1[:], pA_[:], AF.Silu), inc=SSM)
                                Pn.wait("ve", SSM, sv)
                                hv[ht] = Pn.op("ve", TT(hsl[:, ht * 1024:(ht + 1) * 1024],
                                                        tb1[:], pB_[:], ALU.mult), inc=VSM)
                            else:
                                Pn.wait("ve", TSM, tm)
                                Pn.op("ve", TT(tb1[:], pA_[:], srep[:], ALU.mult))
                                vv = Pn.op("ve", TT(tb2[:], pB_[:], srep[:], ALU.mult), inc=VSM)
                                Pn.wait("se", VSM, vv)
                                sv = Pn.op("se", ACT(tb3[:], tb1[:], AF.Silu), inc=SSM)
                                Pn.wait("ve", SSM, sv)
                                hv[ht] = Pn.op("ve", TT(hsl[:, ht * 1024:(ht + 1) * 1024],
                                                        tb3[:], tb2[:], ALU.mult), inc=VSM)
                        vhsl = kb.cnt[VSM]
                        prev_tewm = tewm
                        wv = {}
                        for dh in range(2):
                            if shared:
                                s2 = A3(dr["sw2"], dh * 1024, (D, 128), (128 * D, FT), (1, 1024))
                            else:
                                s2 = A3(dr["w2"], e * FHID * D + dh * 1024, (D, 128), (128 * D, FT), (1, 1024))
                            Pn.wait("gp", TSM, kb.cnt[TSM])
                            w2w = Pn.dma("gp", A3(w2s, 0, (FT * 1024, 128), (1024, FT), (1, 1024)), s2,
                                         sem="D12")
                            Pn.wait("te", "D12", w2w)
                            Pn.wait("te", VSM, vhsl)
                            for ltr in range(8):
                                kk = dh * 8 + ltr
                                pO_ = pU if kk % 2 == 0 else pV
                                if kk >= 2:
                                    Pn.wait("te", VSM, wv[kk - 2])
                                for ht in range(FT):
                                    lhs = hsl[:, ht * 1024 + ltr * 128: ht * 1024 + (ltr + 1) * 128]
                                    Pn.op("te", MM(pO_[:, 0:512], lhs,
                                                   w2s[:, ht * 1024:ht * 1024 + 512],
                                                   start=(ht == 0), stop=(ht == FT - 1)))
                                    Pn.op("te", MM(pO_[:, 512:1024], lhs,
                                                   w2s[:, ht * 1024 + 512:(ht + 1) * 1024],
                                                   start=(ht == 0), stop=(ht == FT - 1)))
                                tm = Pn.tag("te", TSM)
                                Pn.wait("ve", TSM, tm)
                                ac = acc[:, ltr * D + dh * 1024: ltr * D + (dh + 1) * 1024]
                                if shared:
                                    wv[kk] = Pn.op("ve", CPY(ac, pO_[:]), inc=VSM)
                                else:
                                    wv[kk] = Pn.op("ve", TT(ac, ac, pO_[:], ALU.add), inc=VSM)

                with contextlib.ExitStack() as ph3:
                    sb = lambda n, s, d: ph3.enter_context(_sbuf(nc, n, list(s), d))
                    x2t = [sb(f"pEx{j}", (128, D), F32) for j in range(2)]
                    xc = sb("pExc", (128, D), F32)
                    sq = sb("pEsq", (128, D), F32)
                    ot = [sb(f"pEo{j}", (128, D), F32) for j in range(2)]
                    cst = sb("pEcst", (128, 8), F32)
                    vbase = kb.cnt[VSM]
                    spo = {}
                    Pn.wait("sy", VSM, vbase)
                    Pn.wait("sy", TSM, kb.cnt[TSM])
                    Pn.wait("sy", SSM, kb.cnt[SSM])
                    for ltr in range(8):
                        lt = lh * 8 + ltr
                        if ltr >= 2:
                            Pn.wait("sy", VSM, vbase + 4 * (ltr - 2) + 4)
                        wx2 = Pn.dma("sy", x2t[ltr % 2][:], dr["sp_x2"][lt * 128:(lt + 1) * 128, :],
                                     sem=f"D{ltr % 2}")
                        src = acc[:, ltr * D:(ltr + 1) * D]
                        Pn.op("ve", RSUM(cst[:, 0:1], src))
                        Pn.fence("ve")
                        Pn.op("ve", TS(cst[:, 1:2], cst[:, 0:1], 1.0 / D, None, ALU.mult))
                        Pn.fence("ve")
                        vv = Pn.op("ve", TS(xc[:], src, cst[:, 1:2], None, ALU.subtract), inc=VSM)
                        Pn.wait("se", VSM, vv)
                        sv = Pn.op("se", ACT(sq[:], xc[:], AF.Square), inc=SSM)
                        Pn.wait("ve", SSM, sv)
                        vv = Pn.op("ve", RSUM(cst[:, 2:3], sq[:]), inc=VSM)
                        Pn.wait("se", VSM, vv)
                        Pn.op("se", ACT(cst[:, 4:5], cst[:, 2:3], AF.Ln, bias=EPS_LN, scale=1.0 / D))
                        Pn.fence("se")
                        sv = Pn.op("se", ACT(cst[:, 5:6], cst[:, 4:5], AF.Exp, scale=-0.5), inc=SSM)
                        Pn.wait("ve", SSM, sv)
                        vv = Pn.op("ve", TS(xc[:], xc[:], cst[:, 5:6], None, ALU.mult), inc=VSM)
                        Pn.wait("ve", VSM, vv)
                        Pn.wait("ve", f"D{ltr % 2}", wx2)
                        if ltr >= 2:
                            Pn.wait("ve", f"D{19 + ltr % 2}", spo[ltr - 2])
                        vv = Pn.op("ve", TT(ot[ltr % 2][:], x2t[ltr % 2][:], xc[:], ALU.add), inc=VSM)
                        Pn.wait("se", VSM, vv)
                        spo[ltr] = Pn.dma("se", dr["out"][lt * 128:(lt + 1) * 128, :], ot[ltr % 2][:],
                                          sem=f"D{19 + ltr % 2}")
                    finish_phase(Pn, ph, [(TSM, kb.cnt[TSM]), (VSM, kb.cnt[VSM]),
                                          (SSM, kb.cnt[SSM]), ("SQ", kb.cnt["SQ"]),
                                          ("GQ", kb.cnt["GQ"]), ("LD", kb.cnt["LD"])])

        with contextlib.ExitStack() as ph:
            Pn = Plan(kb)
            Pn.wait("gp", "SQ", kb.cnt["SQ"])
            Pn.wait("gp", "D19", kb.cnt["D19"])
            Pn.wait("gp", "D20", kb.cnt["D20"])
            Pn.wait("gp", "PH", kb.cnt["PH"])
            blk = ph.enter_context(nc.Block())
            Pn.run(blk)

    return nc


_CACHE = {}


def _prep_shared(inputs):
    g = lambda n: np.ascontiguousarray(np.asarray(inputs[n], dtype=np.float32))
    o = np.ones((128, 128), np.float32)
    shE0 = np.zeros((128, 128), np.float32)
    shE0[0, 127] = 1.0
    shared = {
        "WxT": np.ascontiguousarray(g("L0_attn_Wx").T),
        "WzT": np.ascontiguousarray(g("L0_attn_Wz").T),
        "WBT": np.ascontiguousarray(g("L0_attn_WB").T),
        "WCT": np.ascontiguousarray(g("L0_attn_WC").T),
        "WthT": np.ascontiguousarray(g("L0_attn_Wth").T),
        "WdlT": np.ascontiguousarray(
            np.concatenate([g("L0_attn_Wdt"), g("L0_attn_Wlam")], axis=0).T),
        "WoutT": np.ascontiguousarray(g("L0_attn_Wout").T),
        "gwT": np.ascontiguousarray(
            g("L0_ffn_gate_w").reshape(16, 128, 8).transpose(1, 0, 2).reshape(128, 128)),
        "w1": np.ascontiguousarray(g("L0_ffn_w1").reshape(NE * D, FHID)),
        "w2": np.ascontiguousarray(g("L0_ffn_w2").reshape(NE * FHID, D)),
        "w3": np.ascontiguousarray(g("L0_ffn_w3").reshape(NE * D, FHID)),
        "sw1": g("L0_ffn_sw1"),
        "sw2": g("L0_ffn_sw2"),
        "sw3": g("L0_ffn_sw3"),
        "Alrep": np.ascontiguousarray(np.tile(g("L0_attn_A_log")[None, :], (128, 1))),
        "ebrep": np.ascontiguousarray(np.tile(g("L0_ffn_expert_bias")[None, :], (128, 1))),
        "bnwB": np.ascontiguousarray(np.tile(g("L0_attn_Bn_w"), (128, 8))),
        "bnwC": np.ascontiguousarray(np.tile(g("L0_attn_Cn_w"), (128, 8))),
        "biasB": np.ascontiguousarray(np.tile(g("L0_attn_B_bias").reshape(1, -1), (128, 1))),
        "biasC": np.ascontiguousarray(np.tile(g("L0_attn_C_bias").reshape(1, -1), (128, 1))),
        "cvals": np.ascontiguousarray(np.tile(np.array([[EPS_LN, 1.0, EPS_RMS, np.pi / 2,
                                                          0.0, 0.0, 0.0, 0.0]], np.float32), (128, 1))),
        "idF": np.eye(128, dtype=np.float32),
        "triL": np.triu(o).astype(np.float32),
        "triUs": np.tril(o, -1).astype(np.float32),
        "maskM": (np.tril(o, -1) * NEG).astype(np.float32),
        "ones128": o,
        "onesrow": np.ones((1, 128), np.float32),
        "onescol": np.ones((128, 1), np.float32),
        "shSup": np.eye(128, k=-1).astype(np.float32),
        "shE0": shE0,
    }
    return shared


def kernel(**inputs):
    x = np.ascontiguousarray(np.asarray(inputs["L0_attn_x"], dtype=np.float32))
    shared = _prep_shared(inputs)
    if "nc" not in _CACHE:
        _CACHE["nc"] = build_nc()
    nc = _CACHE["nc"]

    in_maps = []
    for b in range(8):
        m = dict(shared)
        m["x"] = np.ascontiguousarray(x[b])
        in_maps.append(m)
    res = run_bass_kernel_spmd(nc, in_maps, core_ids=list(range(8)))
    return np.stack([np.asarray(r["out"], dtype=np.float32) for r in res.results], axis=0)
